# version 02 (final)
# baseline (speedup 1.0000x reference)
import math
from contextlib import ExitStack

import numpy as np

import concourse.bass as bass
import concourse.mybir as mybir
from concourse import library_config as libcfg
from concourse.bass_utils import run_bass_kernel_spmd

F32 = mybir.dt.float32
I32 = mybir.dt.int32
U32 = mybir.dt.uint32
AF = mybir.ActivationFunctionType
ALU = mybir.AluOpType
AX = mybir.AxisListType

NCORES = 8
T = 2048
DM = 1024
DC = 8
G = 256
NG = 8
QK = 128
HID = 2048
FF0 = 4096
RK = 256
NH = 16
HD = 64
E = 8
FFE = 4096
CAP = 1024
LN_EPS = 1e-5
KERN_EPS = 1e-6
BIAS_SCALE = float(128 ** 0.5)

STAGE = 4
SUBSTAGE = ""
ENGINES = ("sync", "tensor", "vector", "scalar", "gpsimd")


def _footprint(ap):
    t = ap.tensor
    pairs = [list(p) for p in ap.ap]
    base_p = getattr(t, "base_partition", 0) or 0
    pcount = pairs[0][1]
    p1 = base_p + pcount
    off = ap.offset
    hi = off
    for st, ct in pairs[1:]:
        hi += abs(st) * (ct - 1)
    return (t.name, base_p, p1, off, hi + 1)


def _overlap(f1, f2):
    return (f1[0] == f2[0] and f1[1] < f2[2] and f2[1] < f1[2]
            and f1[3] < f2[4] and f2[3] < f1[4])


class Prog:
    def __init__(self, nc, ctx):
        self.nc = nc
        self._ctx = ctx
        self.steps = {e: [] for e in ENGINES}
        self.sems = {}
        self.sem_count = {}
        self.sem_latest = {}
        self.etime = {e: 0 for e in ENGINES}
        self.known = {e: {} for e in ENGINES}
        self.drained = {e: 0 for e in ENGINES}
        self.live = {e: [] for e in ENGINES}
        self.last_write = {}
        self.last_reads = {}
        self.dma_seq = 0
        self.n_drains = 0
        self.n_ops = 0

    MAX_PHYS = 60

    def sem(self, name):
        if name not in self.sems:
            if not hasattr(self, "_phys"):
                self._phys = []
                self._phys_i = 0
            if len(self._phys) < self.MAX_PHYS:
                h = self._ctx.enter_context(self.nc.semaphore(name))
                self._phys.append(h)
            else:
                h = self._phys[self._phys_i % self.MAX_PHYS]
                self._phys_i += 1
            self.sems[name] = h
            key = id(h)
            self.sem_count.setdefault(key, 0)
            self.sem_latest.setdefault(key, {})
        return self.sems[name]

    def wait(self, engine, name, value=None):
        h = self.sem(name)
        key = id(h)
        v = self.sem_count[key] if value is None else value
        assert v <= self.sem_count[key]
        self.steps[engine].append(lambda eng, h=h, v=v: eng.wait_ge(h, v))
        for src, tme in self.sem_latest[key].items():
            if self.known[engine].get(src, 0) < tme:
                self.known[engine][src] = tme

    def _hazard_check(self, engine, reads, writes):
        rfp = [_footprint(a) for a in reads]
        wfp = [_footprint(a) for a in writes]
        need_drain = False
        for (fp, mode, tme) in self.live[engine]:
            if tme <= self.drained[engine]:
                continue
            for f in rfp:
                if mode == 'w' and _overlap(fp, f):
                    need_drain = True
            for f in wfp:
                if _overlap(fp, f):
                    need_drain = True
        if need_drain:
            self.steps[engine].append(lambda eng: eng.drain())
            self.n_drains += 1
            self.drained[engine] = self.etime[engine]
            self.live[engine] = []
        for f in rfp + wfp:
            for (fp, src_e, tme) in self.last_write.get(f[0], []):
                if src_e != engine and _overlap(fp, f):
                    assert self.known[engine].get(src_e, 0) >= tme, (
                        f"UNSYNCED: {engine} touches {f} written by {src_e}@{tme} "
                        f"known={self.known[engine].get(src_e, 0)}")
        for f in wfp:
            for (fp, src_e, tme) in self.last_reads.get(f[0], []):
                if src_e != engine and _overlap(fp, f):
                    assert self.known[engine].get(src_e, 0) >= tme, (
                        f"UNSYNCED WAR: {engine} writes {f} read by {src_e}@{tme} "
                        f"known={self.known[engine].get(src_e, 0)}")
        return rfp, wfp

    def _register(self, engine, rfp, wfp):
        t = self.etime[engine] = self.etime[engine] + 1
        for fp in rfp:
            self.live[engine].append((fp, 'r', t))
            lst = self.last_reads.setdefault(fp[0], [])
            lst.append((fp, engine, t))
            if len(lst) > 120:
                del lst[:60]
        for fp in wfp:
            self.live[engine].append((fp, 'w', t))
            lst = self.last_write.setdefault(fp[0], [])
            lst.append((fp, engine, t))
            if len(lst) > 120:
                del lst[:60]
        if len(self.live[engine]) > 160:
            del self.live[engine][:80]
        return t

    def op(self, engine, fn, reads=(), writes=(), inc=None, inc_n=1):
        rfp, wfp = self._hazard_check(engine, reads, writes)
        t = self._register(engine, rfp, wfp)
        self.n_ops += 1
        if inc is not None:
            h = self.sem(inc)
            key = id(h)
            self.sem_count[key] += inc_n
            self.sem_latest[key][engine] = t
            self.steps[engine].append(lambda eng, fn=fn, h=h, n=inc_n: fn(eng).then_inc(h, n))
        else:
            self.steps[engine].append(lambda eng, fn=fn: fn(eng))

    def _dma_common(self, engine, rd, wr, inc, check=True):
        self.dma_seq += 1
        venv = f"@d{self.dma_seq}"
        if check:
            self._hazard_check(engine, rd, wr)
        self.etime[venv] = 1
        for a in rd:
            fp = _footprint(a)
            lst = self.last_reads.setdefault(fp[0], [])
            lst.append((fp, venv, 1))
            if len(lst) > 120:
                del lst[:60]
        for a in wr:
            fp = _footprint(a)
            lst = self.last_write.setdefault(fp[0], [])
            lst.append((fp, venv, 1))
            if len(lst) > 120:
                del lst[:60]
        h = self.sem(inc)
        key = id(h)
        self.sem_count[key] += 16
        self.sem_latest[key][venv] = 1
        return h

    def dma(self, engine, out, in_, inc, **kw):
        h = self._dma_common(engine, [in_], [out], inc)
        self.n_ops += 1
        self.steps[engine].append(
            lambda eng, out=out, in_=in_, h=h, kw=kw: eng.dma_start(out=out, in_=in_, **kw).then_inc(h, 16))

    def indirect(self, out, out_offset, in_, in_offset, inc, compute_op=None, extra_reads=(),
                 unchecked_out=False):
        if unchecked_out:
            self._hazard_check("gpsimd", [in_] + list(extra_reads), [])
            h = self._dma_common("gpsimd", [], [out], inc, check=False)
        else:
            h = self._dma_common("gpsimd", [in_] + list(extra_reads), [out], inc)
        kw = {}
        if compute_op is not None:
            kw["compute_op"] = compute_op
        self.n_ops += 1
        self.steps["gpsimd"].append(
            lambda eng, out=out, oo=out_offset, in_=in_, io=in_offset, h=h, kw=kw:
                eng.indirect_dma_start(out=out, out_offset=oo, in_=in_, in_offset=io, **kw).then_inc(h, 16))

    def collective(self, kind, aluop, groups, in_ap, out_ap, inc):
        h = self._dma_common("gpsimd", [in_ap], [out_ap], inc)
        self.sem_count[id(h)] -= 15
        self.steps["gpsimd"].append(
            lambda eng, in_ap=in_ap, out_ap=out_ap, h=h, kind=kind, aluop=aluop, groups=groups:
                eng.collective_compute(kind, aluop, replica_groups=groups,
                                       ins=[in_ap], outs=[out_ap]).then_inc(h, 1))

    def raw(self, engine, fn):
        self.etime[engine] += 1
        self.steps[engine].append(fn)

    def barrier(self, name):
        for e in ENGINES:
            if e in ("vector", "gpsimd"):
                self.op(e, lambda eng: eng.engine_nop(), inc=name)
            else:
                self.op(e, lambda eng: eng.nop(), inc=name)
        for e in ENGINES:
            self.wait(e, name)
            for f in list(self.etime.keys()):
                if f != e:
                    self.known[e][f] = self.etime[f]
            self.drained[e] = self.etime[e]
            self.live[e] = []

    def replay(self, block):
        for ename in ENGINES:
            steps = self.steps[ename]

            def run(eng, steps=steps):
                for s in steps:
                    s(eng)
            getattr(block, ename)(run)


class Builder:
    def __init__(self, nc, stage):
        self.nc = nc
        self.stage = stage
        self.ctx = ExitStack()
        nc._keepalive_ctx = self.ctx
        self.P = Prog(nc, self.ctx)
        self.phase_ctx = None
        self.declare_dram()
        self.declare_global_sbuf()

    def inp(self, name, shape, dt=F32):
        return self.nc.dram_tensor(name, shape, dt, kind="ExternalInput").ap()

    def dram(self, name, shape, dt=F32):
        return self.nc.dram_tensor(name, shape, dt).ap()

    def gsb(self, name, shape, dt=F32):
        return self.ctx.enter_context(self.nc.sbuf_tensor(name, shape, dt))

    def sb(self, name, shape, dt=F32):
        return self.phase_ctx.enter_context(self.nc.sbuf_tensor(name, shape, dt))

    def psum(self, name, shape):
        return self.ctx.enter_context(self.nc.psum_tensor(name, shape, F32))

    def begin(self):
        self.phase_ctx = ExitStack()

    def end(self, barname):
        self.P.barrier(barname)
        self.phase_ctx.close()
        self.phase_ctx = None

    def declare_dram(self):
        inp, dram = self.inp, self.dram
        self.xT = inp("xT", [DM, T])
        self.wh = inp("wh", [DM, 2 * HID])
        self.bh_r = inp("bh", [2 * HID])
        self.wqk = inp("wqk", [DM, QK])
        self.bqk_r = inp("bqk", [QK])
        self.qkw_r = inp("qkw", [4, QK])
        self.qkb_r = inp("qkb", [4, QK])
        self.biasT_r = inp("biasTd", [2, 128, G])
        self.lnw_r = inp("lnw", [DM])
        self.lnb_r = inp("lnb", [DM])
        self.wout0 = inp("wout0", [HID, DM])
        self.bout0_r = inp("bout0", [DM])
        self.u1 = inp("u1", [DM, RK]); self.v1 = inp("v1", [RK, FF0]); self.fb1_r = inp("fb1", [FF0])
        self.u2 = inp("u2", [FF0, RK]); self.v2 = inp("v2", [RK, DM]); self.fb2_r = inp("fb2", [DM])
        self.wq = inp("wq", [DM, DM]); self.wk = inp("wk", [DM, DM]); self.wv = inp("wv", [DM, DM])
        self.wo1 = inp("wo1", [DM, DM]); self.bo1_r = inp("bo1", [DM])
        self.gam_r = inp("gam", [DM]); self.bet_r = inp("bet", [DM])
        self.wg = inp("wg", [DM, E])
        self.w1 = inp("w1", [E, DM, FFE]); self.mb1_r = inp("mb1", [E, FFE])
        self.w2 = inp("w2", [E, FFE, DM]); self.mb2_r = inp("mb2", [E, DM])
        self.maskc_r = inp("maskc", [128, 1])
        self.out = self.nc.dram_tensor("out", [T, DM], F32, kind="ExternalOutput").ap()
        self.vspill = dram("vspill", [NG, 2, 128, HID])
        self.ccl_in = dram("ccl_in", [128, HID])
        self.ccl_out = dram("ccl_out", [2, 128, HID])
        self.mu1d = dram("mu1d", [2, T])
        self.qkd = dram("qkd", [128, T])
        self.linkd = dram("linkd", [NG, 128, 2, 128])
        self.x2sp = dram("x2sp", [DC, 128, T])
        self.cckv_in = dram("cckv_in", [2, 2, 128, T])
        self.cckv_out = dram("cckv_out", [2, 2, 2, 128, T])
        self.ccs_in = dram("ccs_in", [E, T])
        self.ccs_out = dram("ccs_out", [NCORES, E, T])
        self.x3ptm = dram("x3ptm", [T + 1, DM + 8])
        self.x3tm = dram("x3tm", [T, DM])
        self.destd = dram("destd", [E, T], I32)
        self.gvd = dram("gvd", [E, T])
        self.pairsd = dram("pairsd", [E * (CAP + 16), 2])
        self.accd = dram("accd", [T + CAP, DM])

    def declare_global_sbuf(self):
        g = self.gsb
        self.SA = g("SA", [128, DC, T])
        self.SB = g("SB", [128, DC, T])
        self.scr1 = g("scr1", [128, DC, 512])
        self.scr2 = g("scr2", [128, DC, 512])
        self.lnsq = g("lnsq", [128, 2, 512])
        self.mu = g("mu", [1, 512])
        self.r = g("r", [1, 512])
        self.mu2 = g("mu2", [1, 512])
        self.r2 = g("r2", [1, 512])
        self.ones_col = g("ones_col", [128, 1])
        self.ones_row = g("ones_row", [1, 128])
        self.ident = g("ident", [128, 128])
        self.mask_c = g("mask_c", [128, 1])
        self.eps_c = g("eps_c", [128, 2])
        self.pb = [self.psum(f"pb{i}", [128, 512]) for i in range(8)]

    def ln_stats(self, chunks, W, murow, rrow, off, pbs=None, guards=()):
        P = self.P
        pbA, pbB = pbs if pbs is not None else (self.pb[0], self.pb[1])
        P.wait("tensor", "lnV")
        P.wait("tensor", "lnR")
        P.wait("vector", "lnR")
        for gname in guards:
            if gname in P.sems:
                P.wait("tensor", gname)
        for c in range(DC):
            P.op("tensor", lambda t, c=c: t.matmul(
                pbA[0:1, :W], self.ones_col[:, :], chunks[c],
                start=(c == 0), stop=(c == DC - 1), skip_group_check=True),
                reads=[self.ones_col[:, :], chunks[c]], writes=[pbA[0:1, :W]],
                inc="lnA" if c == DC - 1 else None)
        for c in range(DC):
            sq = self.lnsq[:, c % 2, :W]
            P.wait("scalar", "lnB")
            P.wait("scalar", "lnV")
            P.op("scalar", lambda s, c=c, sq=sq: s.activation(out=sq, in_=chunks[c], func=AF.Square),
                 reads=[chunks[c]], writes=[sq], inc="lnS")
            P.wait("tensor", "lnS")
            P.op("tensor", lambda t, c=c, sq=sq: t.matmul(
                pbB[0:1, :W], self.ones_col[:, :], sq,
                start=(c == 0), stop=(c == DC - 1), skip_group_check=True),
                reads=[self.ones_col[:, :], sq], writes=[pbB[0:1, :W]],
                inc="lnB")
        mu_ap = murow[0:1, off:off + W]
        r_ap = rrow[0:1, off:off + W]
        tmp = self.lnsq[0:1, 0, :W]
        P.wait("vector", "lnA")
        P.op("vector", lambda v: v.tensor_scalar_mul(mu_ap, pbA[0:1, :W], 1.0 / DM),
             reads=[pbA[0:1, :W]], writes=[mu_ap], inc="lnV")
        P.wait("vector", "lnB")
        P.op("vector", lambda v: v.tensor_scalar_mul(r_ap, pbB[0:1, :W], 1.0 / DM),
             reads=[pbB[0:1, :W]], writes=[r_ap], inc="lnV")
        P.wait("vector", "lnS")
        P.op("vector", lambda v: v.tensor_mul(tmp, mu_ap, mu_ap),
             reads=[mu_ap], writes=[tmp], inc="lnV")
        P.op("vector", lambda v: v.tensor_sub(r_ap, r_ap, tmp),
             reads=[r_ap, tmp], writes=[r_ap], inc="lnV")
        P.wait("scalar", "lnV")
        P.op("scalar", lambda s: s.activation(out=r_ap, in_=r_ap, func=AF.Sqrt, bias=self.eps_c[0:1, 0:1]),
             reads=[r_ap, self.eps_c[0:1, 0:1]], writes=[r_ap], inc="lnR")
        P.wait("vector", "lnR")
        P.op("vector", lambda v: v.reciprocal(r_ap, r_ap),
             reads=[r_ap], writes=[r_ap], inc="lnV")

    def bcast2(self, murow, rrow, off, W, pbs=None, guards=()):
        P = self.P
        pbA, pbB = pbs if pbs is not None else (self.pb[0], self.pb[1])
        mu_ap = murow[0:1, off:off + W]
        r_ap = rrow[0:1, off:off + W]
        P.wait("tensor", "lnR")
        P.wait("tensor", "lnV")
        for gname in guards:
            if gname in P.sems:
                P.wait("tensor", gname)
        P.op("tensor", lambda t: t.matmul(pbA[:, :W], self.ones_row[:, :], mu_ap,
                                          start=True, stop=True, skip_group_check=True),
             reads=[self.ones_row[:, :], mu_ap], writes=[pbA[:, :W]], inc="bc")
        P.op("tensor", lambda t: t.matmul(pbB[:, :W], self.ones_row[:, :], r_ap,
                                          start=True, stop=True, skip_group_check=True),
             reads=[self.ones_row[:, :], r_ap], writes=[pbB[:, :W]], inc="bc")
        return pbA[:, :W], pbB[:, :W]

    def normalize_into(self, dst_chunks, src_chunks, mu_b, r_b, sem):
        P = self.P
        for c in range(DC):
            P.op("vector", lambda v, c=c: v.tensor_sub(dst_chunks[c], src_chunks[c], mu_b),
                 reads=[src_chunks[c], mu_b], writes=[dst_chunks[c]], inc=sem)
            P.op("vector", lambda v, c=c: v.tensor_mul(dst_chunks[c], dst_chunks[c], r_b),
                 reads=[dst_chunks[c], r_b], writes=[dst_chunks[c]], inc=sem)

    def prologue_and_phase1(self):
        P = self.P
        self.begin()
        sb = self.sb
        bqk_c = sb("bqk_c", [128, 1])
        qkw_c = sb("qkw_c", [128, 4])
        qkb_c = sb("qkb_c", [128, 4])
        bhg_c = sb("bhg_c", [128, 16])

        bout_c = sb("bout_c", [128, DC])
        lnw_c = sb("lnw_c", [128, DC])
        lnb_c = sb("lnb_c", [128, DC])
        biasT = sb("biasT", [128, 2, G])
        cmask = sb("cmask", [128, 2, G])
        for c in range(DC):
            P.dma("sync", self.SA[:, c, :], self.xT[128 * c:128 * (c + 1), :], "dX")
        P.dma("sync", bqk_c[:, 0], self.bqk_r[:], "dW")
        P.dma("sync", qkw_c[:, :], self.qkw_r.transpose([1, 0]), "dW", allow_slow_non_contiguous=True)
        P.dma("sync", qkb_c[:, :], self.qkb_r.transpose([1, 0]), "dW", allow_slow_non_contiguous=True)

        P.dma("sync", biasT[:, :, :], self.biasT_r[:, :, :].transpose([1, 0, 2]), "dW")
        for c in range(DC):
            P.dma("sync", bout_c[:, c], self.bout0_r[128 * c:128 * (c + 1)], "dW")
            P.dma("sync", lnw_c[:, c], self.lnw_r[128 * c:128 * (c + 1)], "dW")
            P.dma("sync", lnb_c[:, c], self.lnb_r[128 * c:128 * (c + 1)], "dW")
        for c in range(16):
            P.dma("sync", bhg_c[:, c], self.bh_r[HID + 128 * c:HID + 128 * (c + 1)], "dW")
        P.dma("sync", self.mask_c[:, :], self.maskc_r[:, :], "dW")
        P.op("vector", lambda v: v.memset(self.ones_col[:, :], 1.0),
             writes=[self.ones_col[:, :]], inc="vc")
        P.op("vector", lambda v: v.memset(self.eps_c[:, 0:1], LN_EPS),
             writes=[self.eps_c[:, 0:1]], inc="vc")
        P.op("vector", lambda v: v.memset(self.eps_c[:, 1:2], KERN_EPS),
             writes=[self.eps_c[:, 1:2]], inc="vc")
        P.op("vector", lambda v: v.memset(self.ones_row[:, :], 1.0),
             writes=[self.ones_row[:, :]], inc="vc")

        P.op("gpsimd", lambda g: g.memset(self.ident[:, :], 1.0), writes=[self.ident[:, :]])
        P.op("gpsimd", lambda g: g.affine_select(
            out=self.ident[:, :], in_=self.ident[:, :], pattern=[[-1, 128]],
            compare_op=ALU.is_equal, fill=0.0, base=0, channel_multiplier=1),
            reads=[self.ident[:, :]], writes=[self.ident[:, :]], inc="gc")
        for kc in range(2):
            P.op("gpsimd", lambda g, kc=kc: g.memset(cmask[:, kc, :], 1.0),
                 writes=[cmask[:, kc, :]])
            P.op("gpsimd", lambda g, kc=kc: g.affine_select(
                out=cmask[:, kc, :], in_=cmask[:, kc, :], pattern=[[1, G]],
                compare_op=ALU.is_ge, fill=0.0, base=-128 * kc, channel_multiplier=-1),
                reads=[cmask[:, kc, :]], writes=[cmask[:, kc, :]], inc="gc")
        P.wait("vector", "dW")
        P.op("vector", lambda v: v.tensor_scalar_mul(biasT[:, :, :], biasT[:, :, :], BIAS_SCALE),
             reads=[biasT[:, :, :]], writes=[biasT[:, :, :]], inc="vc")
        for h in (0, 3):
            P.op("vector", lambda v, h=h: v.tensor_scalar_mul(
                qkw_c[:, h:h + 1], qkw_c[:, h:h + 1], 1.0 / G),
                reads=[qkw_c[:, h:h + 1]], writes=[qkw_c[:, h:h + 1]], inc="vc")
            P.op("vector", lambda v, h=h: v.tensor_scalar_mul(
                qkb_c[:, h:h + 1], qkb_c[:, h:h + 1], 1.0 / G),
                reads=[qkb_c[:, h:h + 1]], writes=[qkb_c[:, h:h + 1]], inc="vc")
        for e in ENGINES:
            P.wait(e, "vc")
            P.wait(e, "gc")
            P.wait(e, "dX")
            P.wait(e, "dW")
        xp = self.scr1
        for tt in range(4):
            off = 512 * tt
            chunks = [self.SA[:, c, 512 * tt:512 * (tt + 1)] for c in range(DC)]
            if tt > 0:
                P.wait("vector", "dM1")
                P.wait("scalar", "dM1")
            self.ln_stats(chunks, 512, self.mu, self.r, 0)
            P.wait("sync", "lnR")
            P.wait("sync", "lnV")
            P.dma("sync", self.mu1d[0, off:off + 512], self.mu[0:1, :], "dM1")
            P.dma("sync", self.mu1d[1, off:off + 512], self.r[0:1, :], "dM1")
            P.wait("tensor", "p1v")
            mu_b, r_b = self.bcast2(self.mu, self.r, 0, 512, (self.pb[2], self.pb[3]), guards=("p1v",))
            P.wait("vector", "bc")
            self.normalize_into([xp[:, c, :] for c in range(DC)],
                                [self.SA[:, c, off:off + 512] for c in range(DC)],
                                mu_b, r_b, "p1v")
            P.wait("tensor", "p1v")
            P.wait("scalar", "p1v")
            self.ln_stats([xp[:, c, :] for c in range(DC)], 512, self.mu2, self.r2, 0,
                          (self.pb[4], self.pb[5]))
            mu2b, r2b = self.bcast2(self.mu2, self.r2, 0, 512, (self.pb[2], self.pb[3]), guards=("p1v",))
            P.wait("vector", "bc")
            self.normalize_into([self.SB[:, c, off:off + 512] for c in range(DC)],
                                [xp[:, c, :] for c in range(DC)], mu2b, r2b, "p1v")
            for c in range(DC):
                P.op("vector", lambda v, c=c, off=off: v.tensor_scalar(
                    out=self.SB[:, c, off:off + 512], in0=self.SB[:, c, off:off + 512],
                    scalar1=lnw_c[:, c:c + 1], scalar2=lnb_c[:, c:c + 1],
                    op0=ALU.mult, op1=ALU.add),
                    reads=[self.SB[:, c, off:off + 512], lnw_c[:, c:c + 1], lnb_c[:, c:c + 1]],
                    writes=[self.SB[:, c, off:off + 512]], inc="p1v")
        if self.stage == 0:
            self.end("bar1")
            return
        whb = [sb(f"whb{i}", [128, 512]) for i in range(2)]
        qkg = sb("qkg", [128, G])
        link_tm = sb("link_tm", [128, 2, 128])
        v_g = self.scr2[:, :, :].rearrange("p a b -> p (a b)")
        S_sum = sb("S_sum", [128, HID])
        subctx = ExitStack()
        wqk_t = subctx.enter_context(self.nc.sbuf_tensor("wqk_t", [128, DC, QK], F32))
        lkf = subctx.enter_context(self.nc.sbuf_tensor("lkf", [128, G], F32))
        for c in range(DC):
            P.dma("sync", wqk_t[:, c, :], self.wqk[128 * c:128 * (c + 1), :], "dWh")
        P.wait("vector", "p1v")
        P.op("vector", lambda v: v.memset(S_sum[:, :], 0.0), writes=[S_sum[:, :]], inc="p1v")
        P.wait("tensor", "dWh")
        P.wait("tensor", "p1v")
        P.wait("tensor", "lnV")
        P.wait("tensor", "lnR")
        P.wait("tensor", "bc")
        P.wait("scalar", "p1v")
        P.wait("vector", "p1v")
        for g in range(NG):
            o = G * g
            pq = self.pb[0]
            P.wait("tensor", "p1lk")
            for c in range(DC):
                P.op("tensor", lambda t, c=c, o=o, pq=pq: t.matmul(
                    pq[:, 0:G], wqk_t[:, c, :], self.SB[:, c, o:o + G],
                    start=(c == 0), stop=(c == DC - 1), skip_group_check=True),
                    reads=[wqk_t[:, c, :], self.SB[:, c, o:o + G]], writes=[pq[:, 0:G]],
                    inc="p1qk" if c == DC - 1 else None)
            P.wait("scalar", "p1qk")
            P.wait("scalar", "dQk")
            P.wait("scalar", "p1v2")
            P.op("scalar", lambda s, o=o, pq=pq: s.activation(
                out=qkg[:, :], in_=pq[:, 0:G], func=AF.Silu, bias=bqk_c[:, 0:1]),
                reads=[pq[:, 0:G], bqk_c[:, 0:1]], writes=[qkg[:, :]], inc="p1s")
            P.wait("gpsimd", "p1s")
            P.dma("gpsimd", self.qkd[:, o:o + G], qkg[:, :], "dQk")
            P.wait("vector", "p1s")
            P.wait("vector", "p1tr")
            P.op("vector", lambda v, o=o: v.tensor_scalar(
                out=lkf[:, :], in0=qkg[:, :], scalar1=qkw_c[:, 3:4],
                scalar2=qkb_c[:, 3:4], op0=ALU.mult, op1=ALU.add),
                reads=[qkg[:, :], qkw_c[:, 3:4], qkb_c[:, 3:4]],
                writes=[lkf[:, :]], inc="p1v2")
            P.wait("tensor", "p1v2")
            P.wait("tensor", "p1v3")
            P.wait("vector", "dLk")
            for tc in range(2):
                P.wait("tensor", "p1v3")
                P.op("tensor", lambda t, tc=tc: t.transpose(
                    self.pb[1][:, 0:128], lkf[:, 128 * tc:128 * (tc + 1)], self.ident[:, :]),
                    reads=[lkf[:, 128 * tc:128 * (tc + 1)], self.ident[:, :]],
                    writes=[self.pb[1][:, 0:128]], inc="p1tr")
                P.wait("vector", "p1tr")
                P.op("vector", lambda v, tc=tc: v.tensor_copy(
                    link_tm[:, tc, :], self.pb[1][:, 0:128]),
                    reads=[self.pb[1][:, 0:128]], writes=[link_tm[:, tc, :]], inc="p1v3")
            P.wait("gpsimd", "p1v3")
            P.dma("gpsimd", self.linkd[g, :, :, :], link_tm[:, :, :], "dLk")
            for tc in range(2):
                oo = o + 128 * tc
                for et in range(4):
                    pv = self.pb[2 + (et % 2)]
                    P.wait("tensor", "p1sv")
                    P.wait("sync", "p1vm")
                    P.wait("sync", "lnV")
                    P.wait("sync", "lnR")
                    P.wait("sync", "bc")
                    P.wait("sync", "dBh")
                    P.dma("sync", self.r2[0:1, :], self.bh_r[512 * et:512 * (et + 1)], "dBh")
                    for c in range(DC):
                        wb = whb[c % 2]
                        P.wait("sync", "p1vmc")
                        P.wait("sync", "dWv")
                        P.dma("sync", wb[:, :],
                              self.wh[128 * c:128 * (c + 1), 512 * et:512 * (et + 1)], "dWv")
                        P.wait("tensor", "dWv")
                        P.op("tensor", lambda t, c=c, oo=oo, pv=pv, wb=wb: t.matmul(
                            pv[:, :], self.SB[:, c, oo:oo + 128], wb[:, :],
                            start=(c == 0), stop=False, skip_group_check=True),
                            reads=[self.SB[:, c, oo:oo + 128], wb[:, :]],
                            writes=[pv[:, :]], inc="p1vmc")
                    P.wait("tensor", "dBh")
                    P.op("tensor", lambda t, et=et, pv=pv: t.matmul(
                        pv[:, :], self.ones_row[:, :], self.r2[0:1, :],
                        start=False, stop=True, skip_group_check=True),
                        reads=[self.ones_row[:, :], self.r2[0:1, :]],
                        writes=[pv[:, :]], inc="p1vm")
                    P.wait("scalar", "p1vm")
                    P.wait("scalar", "p1lk")
                    P.wait("scalar", "dV")
                    P.op("scalar", lambda s, tc=tc, et=et, pv=pv: s.activation(
                        out=v_g[:, 2048 * tc + 512 * et:2048 * tc + 512 * (et + 1)],
                        in_=pv[:, :], func=AF.Silu),
                        reads=[pv[:, :]],
                        writes=[v_g[:, 2048 * tc + 512 * et:2048 * tc + 512 * (et + 1)]],
                        inc="p1sv")
                P.wait("gpsimd", "p1sv")
                P.dma("gpsimd", self.vspill[g, tc, :, :],
                      v_g[:, 2048 * tc:2048 * (tc + 1)], "dV")
            P.wait("tensor", "p1sv")
            for et in range(4):
                plk = self.pb[4 + (et % 2)]
                P.wait("tensor", "p1v3")
                for tc in range(2):
                    P.op("tensor", lambda t, g=g, tc=tc, et=et, plk=plk: t.matmul(
                        plk[:, :], link_tm[:, tc, :],
                        v_g[:, 2048 * tc + 512 * et:2048 * tc + 512 * (et + 1)],
                        start=(tc == 0), stop=(tc == 1), skip_group_check=True),
                        reads=[link_tm[:, tc, :],
                               v_g[:, 2048 * tc + 512 * et:2048 * tc + 512 * (et + 1)]],
                        writes=[plk[:, :]], inc="p1lk" if tc == 1 else None)
                P.wait("vector", "p1lk")
                P.op("vector", lambda v, et=et, plk=plk: v.tensor_add(
                    S_sum[:, 512 * et:512 * (et + 1)], S_sum[:, 512 * et:512 * (et + 1)], plk[:, :]),
                    reads=[S_sum[:, 512 * et:512 * (et + 1)], plk[:, :]],
                    writes=[S_sum[:, 512 * et:512 * (et + 1)]], inc="p1v3")
        P.wait("gpsimd", "p1v3")
        P.dma("gpsimd", self.ccl_in[:, :], S_sum[:, :], "dC")
        P.wait("gpsimd", "dC")
        P.collective("AllGather", ALU.bypass, [[0, 1], [2, 3], [4, 5], [6, 7]],
                     self.ccl_in.flatten(), self.ccl_out.flatten(), "cc")
        P.wait("gpsimd", "cc")
        S_run = S_sum
        P.wait("gpsimd", "p1lk")
        P.dma("gpsimd", S_run[:, :], self.ccl_out[0, :, :], "dC")
        P.wait("vector", "dC")
        P.op("vector", lambda v: v.tensor_scalar_mul(S_run[:, :], S_run[:, :], self.mask_c[:, 0:1]),
             reads=[S_run[:, :], self.mask_c[:, 0:1]], writes=[S_run[:, :]], inc="p1v3")
        subctx.close()
        if SUBSTAGE == "passA":
            self.end("bar1")
            return
        wo_t = sb("wo_t", [128, DM])
        qk4 = sb("qk4", [128, 3, G])
        attnT = sb("attnT", [128, 2, G])
        gated = sb("gated", [128, 2, G])
        xg_t = self.scr1[:, :, :].rearrange("p a b -> p (a b)")

        def xg(c):
            return xg_t[:, 256 * c:256 * (c + 1)]

        def ysb(c):
            return xg_t[:, 2048 + 256 * c:2048 + 256 * (c + 1)]
        xpg = sb("xpg", [128, G])
        P.wait("sync", "p1sv")
        P.wait("sync", "p1v")
        if SUBSTAGE == "g00":
            self.end("bar1")
            return
        ngrun = 1 if SUBSTAGE == "g0" else NG
        for g in range(ngrun):
            o = G * g
            P.wait("gpsimd", "dV")
            if g > 0:
                P.wait("gpsimd", "p1q2")
                P.wait("gpsimd", "dVr")
            for tc in range(2):
                P.dma("gpsimd", v_g[:, 2048 * tc:2048 * (tc + 1)],
                      self.vspill[g, tc, :, :], "dVr")
            if g > 0:
                P.wait("sync", "p1ya")
                P.wait("sync", "dXg")
            for c in range(DC):
                P.dma("sync", xg(c), self.xT[128 * c:128 * (c + 1), o:o + G], "dXg")
            P.wait("sync", "p1q4")
            P.wait("sync", "p1v2")
            P.wait("sync", "dQk")
            P.wait("sync", "dQk2")
            P.dma("sync", qkg[:, :], self.qkd[:, o:o + G], "dQk2")
            P.wait("sync", "p1lk2")
            P.wait("sync", "p1lk")
            P.wait("sync", "p1v3")
            P.wait("sync", "dLk")
            P.wait("sync", "dLk2")
            P.dma("sync", link_tm[:, :, :], self.linkd[g, :, :, :], "dLk2")
            if SUBSTAGE == "g0r":
                for snm in ("dVr", "dXg", "dQk2", "dLk2"):
                    if snm in P.sems:
                        P.wait("gpsimd", snm)
                self.end("bar1")
                return
            P.wait("vector", "dQk2")
            P.wait("vector", "p1sim")
            for h, slot in ((0, 0), (2, 1), (1, 2)):
                P.op("vector", lambda v, h=h, slot=slot, o=o: v.tensor_scalar(
                    out=qk4[:, slot, :], in0=qkg[:, :], scalar1=qkw_c[:, h:h + 1],
                    scalar2=qkb_c[:, h:h + 1], op0=ALU.mult, op1=ALU.add),
                    reads=[qkg[:, :], qkw_c[:, h:h + 1], qkb_c[:, h:h + 1]],
                    writes=[qk4[:, slot, :]], inc="p1q4")
            P.wait("tensor", "p1q4")
            for kc in range(2):
                pS = self.pb[6 + kc]
                if g > 0:
                    P.wait("tensor", "p1gt")
                P.op("tensor", lambda t, kc=kc, pS=pS: t.matmul(
                    pS[:, 0:G], qk4[:, 1, 128 * kc:128 * (kc + 1)], qk4[:, 0, :],
                    start=True, stop=True, skip_group_check=True),
                    reads=[qk4[:, 1, 128 * kc:128 * (kc + 1)], qk4[:, 0, :]],
                    writes=[pS[:, 0:G]], inc="p1sim")
                P.wait("vector", "p1sim")
                P.wait("vector", "p1quad")
                P.op("vector", lambda v, kc=kc, pS=pS: v.tensor_tensor(
                    attnT[:, kc, :], pS[:, 0:G], biasT[:, kc, :], ALU.add),
                    reads=[pS[:, 0:G], biasT[:, kc, :]], writes=[attnT[:, kc, :]], inc="p1v3")
                P.wait("scalar", "p1v3")
                P.op("scalar", lambda s, kc=kc: s.activation(
                    out=attnT[:, kc, :], in_=attnT[:, kc, :], func=AF.Relu),
                    reads=[attnT[:, kc, :]], writes=[attnT[:, kc, :]], inc="p1s2")
                P.wait("vector", "p1s2")
                P.op("vector", lambda v, kc=kc: v.tensor_mul(
                    attnT[:, kc, :], attnT[:, kc, :], cmask[:, kc, :]),
                    reads=[attnT[:, kc, :], cmask[:, kc, :]], writes=[attnT[:, kc, :]], inc="p1v3")
                P.wait("scalar", "p1v3")
                P.op("scalar", lambda s, kc=kc: s.activation(
                    out=attnT[:, kc, :], in_=attnT[:, kc, :], func=AF.Square),
                    reads=[attnT[:, kc, :]], writes=[attnT[:, kc, :]], inc="p1s2")
            if SUBSTAGE == "g0a":
                for snm in ("dVr", "dXg", "dQk2", "dLk2", "dM2"):
                    if snm in P.sems:
                        P.wait("gpsimd", snm)
                self.end("bar1")
                return
            P.wait("tensor", "dVr")
            P.wait("tensor", "p1s2")
            P.wait("tensor", "p1v3")
            P.wait("tensor", "dC")
            nec = 1 if SUBSTAGE == "g0e" else 16
            for ec in range(nec):
                pg = self.pb[0 + ec % 2]
                for c in range(DC):
                    wb = whb[c % 2]
                    P.wait("sync", "p1gc")
                    P.wait("sync", "p1vmc")
                    P.wait("sync", "dWg")
                    P.wait("sync", "dWv")
                    P.dma("sync", wb[:, 0:128],
                          self.wh[128 * c:128 * (c + 1), HID + 128 * ec:HID + 128 * (ec + 1)],
                          "dWg")
                    P.wait("tensor", "dWg")
                    P.op("tensor", lambda t, c=c, ec=ec, o=o, pg=pg, wb=wb: t.matmul(
                        pg[:, 0:G], wb[:, 0:128], self.SB[:, c, o:o + G],
                        start=(c == 0), stop=(c == DC - 1), skip_group_check=True),
                        reads=[wb[:, 0:128], self.SB[:, c, o:o + G]],
                        writes=[pg[:, 0:G]], inc="p1gc")
                P.wait("scalar", "p1gc")
                P.wait("scalar", "p1gt")
                P.op("scalar", lambda s, ec=ec, pg=pg: s.activation(
                    out=gated[:, ec % 2, :], in_=pg[:, 0:G], func=AF.Silu,
                    bias=bhg_c[:, ec:ec + 1]),
                    reads=[pg[:, 0:G], bhg_c[:, ec:ec + 1]], writes=[gated[:, ec % 2, :]],
                    inc="p1sg")
                po = self.pb[6 + (ec % 2)]
                P.op("tensor", lambda t, ec=ec, po=po: t.matmul(
                    po[:, 0:G], v_g[:, 128 * ec:128 * (ec + 1)], attnT[:, 0, :],
                    start=True, stop=False, skip_group_check=True),
                    reads=[v_g[:, 128 * ec:128 * (ec + 1)], attnT[:, 0, :]],
                    writes=[po[:, 0:G]])
                P.op("tensor", lambda t, ec=ec, po=po: t.matmul(
                    po[:, 0:G], v_g[:, 2048 + 128 * ec:2048 + 128 * (ec + 1)], attnT[:, 1, :],
                    start=False, stop=False, skip_group_check=True),
                    reads=[v_g[:, 2048 + 128 * ec:2048 + 128 * (ec + 1)], attnT[:, 1, :]],
                    writes=[po[:, 0:G]])
                P.op("tensor", lambda t, ec=ec, po=po: t.matmul(
                    po[:, 0:G], S_run[:, 128 * ec:128 * (ec + 1)], qk4[:, 2, :],
                    start=False, stop=True, skip_group_check=True),
                    reads=[S_run[:, 128 * ec:128 * (ec + 1)], qk4[:, 2, :]],
                    writes=[po[:, 0:G]], inc="p1quad")
                P.wait("vector", "p1quad")
                P.wait("vector", "p1sg")
                P.op("vector", lambda v, ec=ec, po=po: v.tensor_mul(
                    gated[:, ec % 2, :], gated[:, ec % 2, :], po[:, 0:G]),
                    reads=[gated[:, ec % 2, :], po[:, 0:G]], writes=[gated[:, ec % 2, :]],
                    inc="p1gt")
                P.wait("sync", "p1wmm")
                P.wait("sync", "dWo")
                P.dma("sync", wo_t[:, :], self.wout0[128 * ec:128 * (ec + 1), :], "dWo")
                P.wait("tensor", "dWo")
                P.wait("tensor", "p1gt")
                P.wait("tensor", "p1sg")
                for dmc in range(DC):
                    py = self.pb[2 + dmc // 2]
                    col = 256 * (dmc % 2)
                    P.op("tensor", lambda t, ec=ec, dmc=dmc, py=py, col=col: t.matmul(
                        py[:, col:col + G], wo_t[:, 128 * dmc:128 * (dmc + 1)], gated[:, ec % 2, :],
                        start=(ec == 0 and dmc % 2 == 0), stop=(ec == 15 and dmc % 2 == 1),
                        skip_group_check=True),
                        reads=[wo_t[:, 128 * dmc:128 * (dmc + 1)], gated[:, ec % 2, :]],
                        writes=[py[:, col:col + G]],
                        inc="p1wmm" if dmc == DC - 1 else None)
            if SUBSTAGE in ("g0b", "g0e"):
                for snm in ("dVr", "dXg", "dQk2", "dLk2", "dWg", "dWo", "dM2"):
                    if snm in P.sems:
                        P.wait("gpsimd", snm)
                self.end("bar1")
                return
            P.wait("tensor", "dLk2")
            for et in range(4):
                plk = self.pb[0 + et % 2]
                P.wait("tensor", "p1v4")
                for tc in range(2):
                    P.op("tensor", lambda t, g=g, tc=tc, et=et, plk=plk: t.matmul(
                        plk[:, :], link_tm[:, tc, :],
                        v_g[:, 2048 * tc + 512 * et:2048 * tc + 512 * (et + 1)],
                        start=(tc == 0), stop=(tc == 1), skip_group_check=True),
                        reads=[link_tm[:, tc, :],
                               v_g[:, 2048 * tc + 512 * et:2048 * tc + 512 * (et + 1)]],
                        writes=[plk[:, :]], inc="p1lk2" if tc == 1 else None)
                P.wait("vector", "p1lk2")
                P.op("vector", lambda v, et=et, plk=plk: v.tensor_add(
                    S_run[:, 512 * et:512 * (et + 1)], S_run[:, 512 * et:512 * (et + 1)], plk[:, :]),
                    reads=[S_run[:, 512 * et:512 * (et + 1)], plk[:, :]],
                    writes=[S_run[:, 512 * et:512 * (et + 1)]], inc="p1v4")
            P.wait("scalar", "p1wmm")
            P.wait("scalar", "dXg")
            P.wait("scalar", "p1v5")
            P.wait("vector", "p1wmm")
            P.wait("vector", "p1v5")
            for dmc in range(DC):
                py = self.pb[2 + dmc // 2]
                col = 256 * (dmc % 2)
                P.op("vector", lambda v, dmc=dmc, py=py, col=col: v.tensor_scalar_add(
                    ysb(dmc), py[:, col:col + G], bout_c[:, dmc:dmc + 1]),
                    reads=[py[:, col:col + G], bout_c[:, dmc:dmc + 1]],
                    writes=[ysb(dmc)], inc="p1ye")
            P.wait("sync", "bc")
            P.wait("sync", "dM1")
            P.wait("sync", "dM2")
            P.wait("sync", "lnV")
            P.wait("sync", "lnR")
            P.dma("sync", self.mu[0:1, 256:256 + G], self.mu1d[0, o:o + G], "dM2")
            P.dma("sync", self.r[0:1, 256:256 + G], self.mu1d[1, o:o + G], "dM2")
            P.wait("tensor", "p1ye")
            P.wait("tensor", "p1v4")
            P.wait("tensor", "dM2")
            mu_b, r_b = self.bcast2(self.mu, self.r, 256, G, (self.pb[0], self.pb[1]), guards=("p1v4", "p1v5"))
            P.wait("vector", "bc")
            P.wait("vector", "p1ye")
            P.wait("vector", "dXg")
            for dmc in range(DC):
                P.op("vector", lambda v, dmc=dmc, mu_b=mu_b: v.tensor_sub(
                    xpg[:, :], xg(dmc), mu_b),
                    reads=[xg(dmc), mu_b], writes=[xpg[:, :]], inc="p1v4")
                P.op("vector", lambda v, r_b=r_b: v.tensor_mul(xpg[:, :], xpg[:, :], r_b),
                     reads=[xpg[:, :], r_b], writes=[xpg[:, :]], inc="p1v4")
                P.op("vector", lambda v, dmc=dmc: v.tensor_add(ysb(dmc), ysb(dmc), xpg[:, :]),
                     reads=[ysb(dmc), xpg[:, :]], writes=[ysb(dmc)], inc="p1v4")
            P.wait("tensor", "p1v4")
            P.wait("scalar", "p1v4")
            P.wait("vector", "dM1")
            P.wait("vector", "dM2")
            P.wait("scalar", "dM1")
            P.wait("scalar", "dM2")
            self.ln_stats([ysb(c) for c in range(DC)], G, self.mu, self.r, 0,
                          (self.pb[0], self.pb[1]))
            mu_b2, r_b2 = self.bcast2(self.mu, self.r, 0, G, (self.pb[0], self.pb[1]))
            P.wait("vector", "bc")
            for dmc in range(DC):
                P.op("vector", lambda v, dmc=dmc, mu_b2=mu_b2: v.tensor_sub(
                    xpg[:, :], ysb(dmc), mu_b2),
                    reads=[ysb(dmc), mu_b2], writes=[xpg[:, :]], inc="p1v5")
                P.op("vector", lambda v, r_b2=r_b2: v.tensor_mul(xpg[:, :], xpg[:, :], r_b2),
                     reads=[xpg[:, :], r_b2], writes=[xpg[:, :]], inc="p1v5")
                P.op("vector", lambda v, dmc=dmc, o=o: v.tensor_add(
                    self.SA[:, dmc, o:o + G], self.SA[:, dmc, o:o + G], xpg[:, :]),
                    reads=[self.SA[:, dmc, o:o + G], xpg[:, :]],
                    writes=[self.SA[:, dmc, o:o + G]], inc="p1v5")
            P.wait("sync", "p1v5")
            P.op("sync", lambda s: s.nop(), inc="p1ya")
            P.wait("gpsimd", "p1lk2")
            P.wait("gpsimd", "p1quad")
            P.op("gpsimd", lambda gg: gg.engine_nop(), inc="p1q2")
        self.end("bar1")

    def emit_stream_out(self, buf):
        P = self.P
        outT = self.out.rearrange("t d -> d t")
        for c in range(DC):
            P.wait("sync", "dOut")
            P.dma("sync", outT[128 * c:128 * (c + 1), :], buf[:, c, :], "dOut", allow_slow_non_contiguous=True)
        P.wait("sync", "dOut")
        P.wait("gpsimd", "dOut")

    def phase2_l0ffn(self):
        P = self.P
        self.begin()
        sb = self.sb
        u1t = sb("u1t", [128, DC, RK])
        v2t = sb("v2t", [128, 2, DM])
        v1t = [sb(f"v1t{i}", [128, 2, 128]) for i in range(2)]
        u2t = [sb(f"u2t{i}", [128, RK]) for i in range(2)]
        fb1_c = sb("fb1_c", [128, FF0 // 128])
        fb2_c = sb("fb2_c", [128, DC])
        Pfm = sb("Pfm", [128, 2, 512])
        Hfm = [sb(f"Hfm{i}", [128, 512]) for i in range(2)]
        xp = self.scr1
        ofm = self.scr2
        for c in range(DC):
            P.dma("sync", u1t[:, c, :], self.u1[128 * c:128 * (c + 1), :], "dW2")
            P.dma("sync", fb2_c[:, c], self.fb2_r[128 * c:128 * (c + 1)], "dW2")
        for rc in range(2):
            P.dma("sync", v2t[:, rc, :], self.v2[128 * rc:128 * (rc + 1), :], "dW2")
        for c in range(FF0 // 128):
            P.dma("sync", fb1_c[:, c], self.fb1_r[128 * c:128 * (c + 1)], "dW2")
        P.wait("tensor", "dW2")
        P.wait("scalar", "dW2")
        P.wait("vector", "dW2")
        for tt in range(4):
            off = 512 * tt
            chunks = [self.SA[:, c, off:off + 512] for c in range(DC)]
            self.ln_stats(chunks, 512, self.mu, self.r, 0, guards=("p2v", "p2v4", "p3v", "p3e2", "p3e3", "p4v"))
            mu_b, r_b = self.bcast2(self.mu, self.r, 0, 512, guards=("p2v", "p2v4", "p3v", "p3e2", "p3e3", "p4v"))
            P.wait("vector", "bc")
            P.wait("vector", "p2v")
            self.normalize_into([xp[:, c, :] for c in range(DC)], chunks, mu_b, r_b, "p2v")
            P.wait("tensor", "p2v")
            for rc in range(2):
                pp = self.pb[2 + rc]
                P.wait("tensor", "p2v2")
                for c in range(DC):
                    P.op("tensor", lambda t, c=c, rc=rc, pp=pp: t.matmul(
                        pp[:, :], u1t[:, c, 128 * rc:128 * (rc + 1)], xp[:, c, :],
                        start=(c == 0), stop=(c == DC - 1), skip_group_check=True),
                        reads=[u1t[:, c, 128 * rc:128 * (rc + 1)], xp[:, c, :]],
                        writes=[pp[:, :]], inc="p2P" if c == DC - 1 else None)
                P.wait("vector", "p2P")
                P.op("vector", lambda v, rc=rc, pp=pp: v.tensor_copy(Pfm[:, rc, :], pp[:, :]),
                     reads=[pp[:, :]], writes=[Pfm[:, rc, :]], inc="p2v2")
            pq = [self.pb[4], self.pb[5]]
            P.wait("tensor", "p2v2")
            P.wait("tensor", "p2v3")
            for fc in range(FF0 // 128):
                b = fc % 2
                P.wait("sync", "p2hm")
                P.wait("sync", "p2h")
                P.wait("sync", "dW2b")
                for rc in range(2):
                    P.dma("sync", v1t[b][:, rc, :],
                          self.v1[128 * rc:128 * (rc + 1), 128 * fc:128 * (fc + 1)], "dW2b")
                P.dma("sync", u2t[b][:, :], self.u2[128 * fc:128 * (fc + 1), :], "dW2b")
                ph = self.pb[6 + b]
                P.wait("tensor", "dW2b")
                P.wait("tensor", "p2hs")
                for rc in range(2):
                    P.op("tensor", lambda t, rc=rc, b=b, ph=ph: t.matmul(
                        ph[:, :], v1t[b][:, rc, :], Pfm[:, rc, :],
                        start=(rc == 0), stop=(rc == 1), skip_group_check=True),
                        reads=[v1t[b][:, rc, :], Pfm[:, rc, :]], writes=[ph[:, :]],
                        inc="p2h" if rc == 1 else None)
                P.wait("scalar", "p2h")
                P.wait("scalar", "p2hm")
                P.op("scalar", lambda s, fc=fc, b=b, ph=ph: s.activation(
                    out=Hfm[b][:, :], in_=ph[:, :], func=AF.Gelu_apprx_tanh,
                    bias=fb1_c[:, fc:fc + 1]),
                    reads=[ph[:, :], fb1_c[:, fc:fc + 1]], writes=[Hfm[b][:, :]], inc="p2hs")
                P.wait("tensor", "p2hs")
                for rc in range(2):
                    P.op("tensor", lambda t, rc=rc, fc=fc, b=b, pq=pq: t.matmul(
                        pq[rc][:, :], u2t[b][:, 128 * rc:128 * (rc + 1)], Hfm[b][:, :],
                        start=(fc == 0), stop=(fc == FF0 // 128 - 1), skip_group_check=True),
                        reads=[u2t[b][:, 128 * rc:128 * (rc + 1)], Hfm[b][:, :]],
                        writes=[pq[rc][:, :]],
                        inc="p2hm" if rc == 1 else None)
            P.wait("vector", "p2hm")
            for rc in range(2):
                P.op("vector", lambda v, rc=rc, pq=pq: v.tensor_copy(Pfm[:, rc, :], pq[rc][:, :]),
                     reads=[pq[rc][:, :]], writes=[Pfm[:, rc, :]], inc="p2v3")
            P.wait("tensor", "p2v3")
            for dmc in range(DC):
                po = self.pb[2 + dmc % 2]
                P.wait("tensor", "p2oe")
                for rc in range(2):
                    P.op("tensor", lambda t, rc=rc, dmc=dmc, po=po: t.matmul(
                        po[:, :], v2t[:, rc, 128 * dmc:128 * (dmc + 1)], Pfm[:, rc, :],
                        start=(rc == 0), stop=(rc == 1), skip_group_check=True),
                        reads=[v2t[:, rc, 128 * dmc:128 * (dmc + 1)], Pfm[:, rc, :]],
                        writes=[po[:, :]], inc="p2o" if rc == 1 else None)
                P.wait("vector", "p2o")
                P.wait("vector", "p2v4")
                P.op("vector", lambda v, dmc=dmc, po=po: v.tensor_scalar_add(
                    ofm[:, dmc, :], po[:, :], fb2_c[:, dmc:dmc + 1]),
                    reads=[po[:, :], fb2_c[:, dmc:dmc + 1]], writes=[ofm[:, dmc, :]],
                    inc="p2oe")
            P.wait("tensor", "p2oe")
            P.wait("vector", "p2oe")
            P.wait("scalar", "p2oe")
            self.ln_stats([ofm[:, c, :] for c in range(DC)], 512, self.mu2, self.r2, 0, guards=("p2v", "p2v4"))
            mu_b2, r_b2 = self.bcast2(self.mu2, self.r2, 0, 512, guards=("p2v", "p2v4", "p3e2", "p3e3"))
            P.wait("vector", "bc")
            self.normalize_into([xp[:, c, :] for c in range(DC)],
                                [ofm[:, c, :] for c in range(DC)], mu_b2, r_b2, "p2v4")
            for dmc in range(DC):
                P.op("vector", lambda v, dmc=dmc, off=off: v.tensor_add(
                    self.SA[:, dmc, off:off + 512], self.SA[:, dmc, off:off + 512],
                    xp[:, dmc, :]),
                    reads=[self.SA[:, dmc, off:off + 512], xp[:, dmc, :]],
                    writes=[self.SA[:, dmc, off:off + 512]], inc="p2v4")
            P.wait("tensor", "p2v4")
            P.wait("scalar", "p2v4")
        self.end("bar2")

    def phase3_l1attn(self):
        P = self.P
        self.begin()
        sb = self.sb
        bo1_c = sb("bo1_c", [128, DC])
        gam_c = sb("gam_c", [128, DC])
        bet_c = sb("bet_c", [128, DC])
        for c in range(DC):
            P.dma("sync", bo1_c[:, c], self.bo1_r[128 * c:128 * (c + 1)], "dW3")
            P.dma("sync", gam_c[:, c], self.gam_r[128 * c:128 * (c + 1)], "dW3")
            P.dma("sync", bet_c[:, c], self.bet_r[128 * c:128 * (c + 1)], "dW3")
        for tt in range(4):
            off = 512 * tt
            chunks = [self.SA[:, c, off:off + 512] for c in range(DC)]
            self.ln_stats(chunks, 512, self.mu, self.r, 0, guards=("p2v", "p2v4", "p3v", "p3e2", "p3e3", "p4v"))
            mu_b, r_b = self.bcast2(self.mu, self.r, 0, 512, guards=("p2v", "p2v4", "p3v", "p3e2", "p3e3", "p4v"))
            P.wait("vector", "bc")
            self.normalize_into([self.SB[:, c, off:off + 512] for c in range(DC)],
                                chunks, mu_b, r_b, "p3v")
        P.wait("gpsimd", "p3v")
        for c in range(DC):
            P.dma("gpsimd", self.x2sp[c, :, :], self.SA[:, c, :], "dX2")
        P.wait("vector", "dX2")
        P.op("vector", lambda v: v.memset(self.SA[:, :, :], 0.0),
             writes=[self.SA[:, :, :]], inc="p3z")
        scr1f = self.scr1[:, :, :].rearrange("p a b -> p (a b)")
        scr2f = self.scr2[:, :, :].rearrange("p a b -> p (a b)")
        af = scr1f[:, 0:2048]
        q_hp = scr1f[:, 2048:4096]
        sT = scr2f[:, 0:1024]
        wo_t = scr2f[:, 1024:2048]
        k_own = sb("k_own", [128, T])
        k_par = sb("k_par", [128, T])
        v_own = sb("v_own", [128, 16, 130])
        v_par = sb("v_par", [128, 16, 130])
        wtile = [sb("wtile0", [128, 128]), sb("wtile1", [128, 128])]
        recip = sb("recip", [128, 2])
        ao_tm = sb("ao_tm", [128, 2, 64])
        P.op("vector", lambda v: v.memset(v_own[:, :, :], 1.0), writes=[v_own[:, :, :]], inc="p3z")
        P.op("vector", lambda v: v.memset(v_par[:, :, :], 1.0), writes=[v_par[:, :, :]], inc="p3z")
        P.wait("tensor", "p3z")
        P.wait("tensor", "p3v")
        P.wait("scalar", "p3z")
        P.wait("scalar", "p3v")
        for hp in range(NH // 2):
            fo = 128 * hp
            P.wait("scalar", "dKV2")
            P.wait("scalar", "p3vk")
            P.wait("scalar", "p3sc")
            for kind, wsrc in (("q", self.wq), ("k", self.wk)):
                for tt in range(4):
                    offq = 512 * tt
                    pq1 = self.pb[2 + tt % 2]
                    P.wait("tensor", "p3qe")
                    for c in range(DC):
                        wt_ = wtile[c % 2]
                        P.wait("sync", "p3mmc")
                        P.wait("sync", "dWq")
                        P.dma("sync", wt_[:, :], wsrc[128 * c:128 * (c + 1), fo:fo + 128], "dWq")
                        P.wait("tensor", "dWq")
                        P.op("tensor", lambda t, c=c, offq=offq, pq1=pq1, wt_=wt_: t.matmul(
                            pq1[:, :], wt_[:, :], self.SB[:, c, offq:offq + 512],
                            start=(c == 0), stop=(c == DC - 1), skip_group_check=True),
                            reads=[wt_[:, :], self.SB[:, c, offq:offq + 512]],
                            writes=[pq1[:, :]], inc="p3mmc")
                    P.wait("scalar", "p3mmc")
                    if kind == "q":
                        P.wait("scalar", "p3vk")
                        P.op("scalar", lambda s, offq=offq, pq1=pq1: s.activation(
                            out=q_hp[:, offq:offq + 512], in_=pq1[:, :], func=AF.Copy),
                            reads=[pq1[:, :]], writes=[q_hp[:, offq:offq + 512]], inc="p3qe")
                    else:
                        P.op("scalar", lambda s, offq=offq, pq1=pq1: s.activation(
                            out=k_own[:, offq:offq + 512], in_=pq1[:, :], func=AF.Copy,
                            scale=0.125),
                            reads=[pq1[:, :]], writes=[k_own[:, offq:offq + 512]], inc="p3qe")
            for tcn in range(16):
                to = 128 * tcn
                pv = self.pb[2 + tcn % 2]
                P.wait("tensor", "p3qe")
                for c in range(DC):
                    wt_ = wtile[c % 2]
                    P.wait("sync", "p3mmc")
                    P.wait("sync", "dWq")
                    P.dma("sync", wt_[:, :], self.wv[128 * c:128 * (c + 1), fo:fo + 128], "dWq")
                    P.wait("tensor", "dWq")
                    P.op("tensor", lambda t, c=c, to=to, pv=pv, wt_=wt_: t.matmul(
                        pv[:, 0:128], self.SB[:, c, to:to + 128], wt_[:, :],
                        start=(c == 0), stop=(c == DC - 1), skip_group_check=True),
                        reads=[self.SB[:, c, to:to + 128], wt_[:, :]],
                        writes=[pv[:, 0:128]], inc="p3mmc")
                P.wait("scalar", "p3mmc")
                P.op("scalar", lambda s, tcn=tcn, pv=pv: s.activation(
                    out=v_own[:, tcn, 0:64], in_=pv[:, 0:64], func=AF.Copy),
                    reads=[pv[:, 0:64]], writes=[v_own[:, tcn, 0:64]], inc="p3qe")
                P.op("scalar", lambda s, tcn=tcn, pv=pv: s.activation(
                    out=v_own[:, tcn, 65:129], in_=pv[:, 64:128], func=AF.Copy),
                    reads=[pv[:, 64:128]], writes=[v_own[:, tcn, 65:129]], inc="p3qe")
            ccp = hp % 2
            if hp >= 2:
                P.wait("gpsimd", "ccK")
            P.wait("gpsimd", "p3qe")
            P.wait("gpsimd", "p3z")
            P.wait("gpsimd", "dKV2")
            P.dma("gpsimd", self.cckv_in[ccp, 0, :, :], k_own[:, :], "dKV")
            P.dma("gpsimd", self.cckv_in[ccp, 1, :, 0:1024], v_own[:, :, 0:64], "dKV")
            P.dma("gpsimd", self.cckv_in[ccp, 1, :, 1024:2048], v_own[:, :, 65:129], "dKV")
            P.wait("gpsimd", "dKV")
            P.collective("AllGather", ALU.bypass, [[0, 1], [2, 3], [4, 5], [6, 7]],
                         self.cckv_in[ccp].flatten(), self.cckv_out[ccp].flatten(), "ccK")
            P.wait("gpsimd", "ccK")
            if hp > 0:
                P.wait("gpsimd", "p3att")
                P.wait("gpsimd", "p3vk")
                P.wait("gpsimd", "p3sc")
                P.wait("gpsimd", "dKV2")
            P.dma("gpsimd", k_par[:, :], self.cckv_out[ccp, 0, 0, :, :], "dKV2")
            P.dma("gpsimd", v_par[:, :, 0:64], self.cckv_out[ccp, 0, 1, :, 0:1024], "dKV2")
            P.wait("gpsimd", "dKV2")
            P.dma("gpsimd", v_par[:, :, 65:129], self.cckv_out[ccp, 0, 1, :, 1024:2048], "dKV2")
            P.dma("gpsimd", k_own[:, :], self.cckv_out[ccp, 1, 0, :, :], "dKV2")
            P.dma("gpsimd", v_own[:, :, 0:64], self.cckv_out[ccp, 1, 1, :, 0:1024], "dKV2")
            P.wait("gpsimd", "dKV2")
            P.dma("gpsimd", v_own[:, :, 65:129], self.cckv_out[ccp, 1, 1, :, 1024:2048], "dKV2")
            P.wait("tensor", "dKV2")
            ksrc = [(k_par, v_par), (k_own, v_own)]
            for qt in range(4):
                oq = 512 * qt
                P.wait("tensor", "p3ao")
                P.wait("tensor", "p3rc")
                for kc in range(32):
                    kt, vt = ksrc[kc // 16]
                    kcl = kc % 16
                    pA = self.pb[0 + (kc % 2)]
                    pB = self.pb[2 + (kc % 2)]
                    if kc >= 1:
                        P.wait("tensor", "p3r")
                    P.op("tensor", lambda t, kt=kt, kcl=kcl, oq=oq, pA=pA: t.matmul(
                        pA[:, :], kt[0:64, 128 * kcl:128 * (kcl + 1)], q_hp[0:64, oq:oq + 512],
                        start=True, stop=True, skip_group_check=True, tile_position=(0, 0)),
                        reads=[kt[0:64, 128 * kcl:128 * (kcl + 1)], q_hp[0:64, oq:oq + 512]],
                        writes=[pA[:, :]], inc="p3sc")
                    P.op("tensor", lambda t, kt=kt, kcl=kcl, oq=oq, pB=pB: t.matmul(
                        pB[:, :], kt[64:128, 128 * kcl:128 * (kcl + 1)], q_hp[64:128, oq:oq + 512],
                        start=True, stop=True, skip_group_check=True, tile_position=(64, 0)),
                        reads=[kt[64:128, 128 * kcl:128 * (kcl + 1)], q_hp[64:128, oq:oq + 512]],
                        writes=[pB[:, :]], inc="p3sc")
                    P.wait("scalar", "p3sc")
                    P.wait("scalar", "p3vk")
                    P.op("scalar", lambda s, pA=pA: s.activation(
                        out=sT[:, 0:512], in_=pA[:, :], func=AF.Relu),
                        reads=[pA[:, :]], writes=[sT[:, 0:512]], inc="p3r")
                    P.wait("vector", "p3sc")
                    P.wait("vector", "p3vk")
                    P.op("vector", lambda v, pB=pB: v.tensor_relu(sT[:, 512:1024], pB[:, :]),
                         reads=[pB[:, :]], writes=[sT[:, 512:1024]], inc="p3r")
                    P.wait("tensor", "p3r")
                    for qc in range(4):
                        po = self.pb[4 + qc // 2]
                        colA = 260 * (qc % 2)
                        P.op("tensor", lambda t, qc=qc, kc=kc, vt=vt, po=po, colA=colA, kcl=kcl: t.matmul(
                            po[:, colA:colA + 65], sT[:, 128 * qc:128 * (qc + 1)],
                            vt[:, kcl, 0:65],
                            start=(kc == 0 and qc % 2 == 0), stop=False, skip_group_check=True),
                            reads=[sT[:, 128 * qc:128 * (qc + 1)], vt[:, kcl, 0:65]],
                            writes=[po[:, colA:colA + 65]])
                        P.op("tensor", lambda t, qc=qc, kc=kc, vt=vt, po=po, colA=colA, kcl=kcl: t.matmul(
                            po[:, colA + 65:colA + 130], sT[:, 512 + 128 * qc:512 + 128 * (qc + 1)],
                            vt[:, kcl, 65:130],
                            start=False, stop=(kc == 31), skip_group_check=True),
                            reads=[sT[:, 512 + 128 * qc:512 + 128 * (qc + 1)], vt[:, kcl, 65:130]],
                            writes=[po[:, colA + 65:colA + 130]],
                            inc="p3vk" if qc == 3 else None)
                P.wait("scalar", "p3vk")
                P.wait("vector", "p3vk")
                for qc in range(4):
                    po = self.pb[4 + qc // 2]
                    colA = 260 * (qc % 2)
                    P.wait("vector", "p3rc")
                    P.op("vector", lambda v, po=po, colA=colA: v.tensor_scalar_add(
                        recip[:, 0:1], po[:, colA + 64:colA + 65], KERN_EPS),
                        reads=[po[:, colA + 64:colA + 65]], writes=[recip[:, 0:1]], inc="p3rc")
                    P.op("vector", lambda v, po=po, colA=colA: v.tensor_scalar_add(
                        recip[:, 1:2], po[:, colA + 129:colA + 130], KERN_EPS),
                        reads=[po[:, colA + 129:colA + 130]], writes=[recip[:, 1:2]], inc="p3rc")
                    P.op("vector", lambda v: v.reciprocal(recip[:, :], recip[:, :]),
                         reads=[recip[:, :]], writes=[recip[:, :]], inc="p3rc")
                    P.wait("vector", "p3tr")
                    P.op("vector", lambda v, po=po, colA=colA: v.tensor_scalar_mul(
                        ao_tm[:, 0, :], po[:, colA:colA + 64], recip[:, 0:1]),
                        reads=[po[:, colA:colA + 64], recip[:, 0:1]],
                        writes=[ao_tm[:, 0, :]], inc="p3ao")
                    P.op("vector", lambda v, po=po, colA=colA: v.tensor_scalar_mul(
                        ao_tm[:, 1, :], po[:, colA + 65:colA + 129], recip[:, 1:2]),
                        reads=[po[:, colA + 65:colA + 129], recip[:, 1:2]],
                        writes=[ao_tm[:, 1, :]], inc="p3ao")
                    P.wait("tensor", "p3ao")
                    for hh in range(2):
                        P.wait("tensor", "p3afe")
                        P.op("tensor", lambda t, hh=hh: t.transpose(
                            self.pb[6][0:64, 0:128], ao_tm[:, hh, :], self.ident[:, :]),
                            reads=[ao_tm[:, hh, :], self.ident[:, :]],
                            writes=[self.pb[6][0:64, 0:128]], inc="p3tr")
                        P.wait("vector", "p3tr")
                        P.op("vector", lambda v, hh=hh, oq=oq, qc=qc: v.tensor_copy(
                            af[64 * hh:64 * (hh + 1), oq + 128 * qc:oq + 128 * (qc + 1)],
                            self.pb[6][0:64, 0:128]),
                            reads=[self.pb[6][0:64, 0:128]],
                            writes=[af[64 * hh:64 * (hh + 1), oq + 128 * qc:oq + 128 * (qc + 1)]],
                            inc="p3afe")
            P.wait("sync", "p3wp")
            P.wait("sync", "p3r")
            P.dma("sync", wo_t[:, :], self.wo1[fo:fo + 128, :], "dWo1")
            P.wait("tensor", "dWo1")
            P.wait("tensor", "p3afe")
            for dmc in range(DC):
                for qt in range(4):
                    po = self.pb[0 + (qt % 2)]
                    P.wait("tensor", "p3ws")
                    P.op("tensor", lambda t, dmc=dmc, qt=qt, po=po: t.matmul(
                        po[:, :], wo_t[:, 128 * dmc:128 * (dmc + 1)], af[:, 512 * qt:512 * (qt + 1)],
                        start=True, stop=True, skip_group_check=True),
                        reads=[wo_t[:, 128 * dmc:128 * (dmc + 1)], af[:, 512 * qt:512 * (qt + 1)]],
                        writes=[po[:, :]], inc="p3wp")
                    P.wait("vector", "p3wp")
                    P.op("vector", lambda v, dmc=dmc, qt=qt, po=po: v.tensor_add(
                        self.SA[:, dmc, 512 * qt:512 * (qt + 1)],
                        self.SA[:, dmc, 512 * qt:512 * (qt + 1)], po[:, :]),
                        reads=[self.SA[:, dmc, 512 * qt:512 * (qt + 1)], po[:, :]],
                        writes=[self.SA[:, dmc, 512 * qt:512 * (qt + 1)]], inc="p3ws")
            P.wait("gpsimd", "p3ws")
            P.op("gpsimd", lambda g: g.engine_nop(), inc="p3att")
        ytmp = self.scr1
        x2t = self.scr2
        P.wait("vector", "p3ws")
        P.wait("vector", "p3vk")
        P.wait("vector", "p3afe")
        P.wait("vector", "dW3")
        P.wait("scalar", "dW3")
        P.wait("vector", "p3qe")
        P.wait("scalar", "p3qe")
        P.wait("tensor", "p3qe")
        P.wait("vector", "p3ao")
        P.wait("scalar", "p3ao")
        P.wait("vector", "p3rc")
        P.wait("scalar", "p3rc")
        P.wait("tensor", "p3ao")
        P.wait("vector", "dWo1")
        P.wait("scalar", "dWo1")
        P.wait("tensor", "dWo1")
        P.wait("vector", "dX2r")
        P.wait("scalar", "dX2r")
        P.wait("tensor", "dX2r")
        P.wait("sync", "p3ws")
        P.wait("sync", "p3vk")
        P.wait("scalar", "p3ws")
        P.wait("sync", "dX2")
        P.wait("sync", "dWo1")
        P.wait("sync", "p3wp")
        P.wait("sync", "p3sc")
        P.wait("sync", "p3rc")
        P.wait("sync", "p3ao")
        for tt in range(4):
            off = 512 * tt
            P.wait("sync", "dX2r")
            for c in range(DC):
                P.dma("sync", x2t[:, c, :], self.x2sp[c, :, off:off + 512], "dX2r")
            P.wait("vector", "p3e3")
            for c in range(DC):
                P.op("vector", lambda v, c=c, off=off: v.tensor_scalar_add(
                    ytmp[:, c, :], self.SA[:, c, off:off + 512], bo1_c[:, c:c + 1]),
                    reads=[self.SA[:, c, off:off + 512], bo1_c[:, c:c + 1]],
                    writes=[ytmp[:, c, :]], inc="p3e1")
                P.op("vector", lambda v, c=c, off=off: v.tensor_add(
                    ytmp[:, c, :], ytmp[:, c, :], self.SB[:, c, off:off + 512]),
                    reads=[ytmp[:, c, :], self.SB[:, c, off:off + 512]],
                    writes=[ytmp[:, c, :]], inc="p3e1")
            P.wait("tensor", "p3e1")
            P.wait("scalar", "p3e1")
            self.ln_stats([ytmp[:, c, :] for c in range(DC)], 512, self.mu, self.r, 0, guards=("p3e2", "p3e3", "bc"))
            mu_b, r_b = self.bcast2(self.mu, self.r, 0, 512, guards=("p2v", "p2v4", "p3v", "p3e2", "p3e3", "p4v"))
            P.wait("vector", "bc")
            for c in range(DC):
                P.op("vector", lambda v, c=c, mu_b=mu_b: v.tensor_sub(
                    ytmp[:, c, :], ytmp[:, c, :], mu_b),
                    reads=[ytmp[:, c, :], mu_b], writes=[ytmp[:, c, :]], inc="p3e2")
                P.op("vector", lambda v, c=c, r_b=r_b: v.tensor_mul(
                    ytmp[:, c, :], ytmp[:, c, :], r_b),
                    reads=[ytmp[:, c, :], r_b], writes=[ytmp[:, c, :]], inc="p3e2")
                P.op("vector", lambda v, c=c: v.tensor_scalar(
                    out=ytmp[:, c, :], in0=ytmp[:, c, :], scalar1=gam_c[:, c:c + 1],
                    scalar2=bet_c[:, c:c + 1], op0=ALU.mult, op1=ALU.add),
                    reads=[ytmp[:, c, :], gam_c[:, c:c + 1], bet_c[:, c:c + 1]],
                    writes=[ytmp[:, c, :]], inc="p3e2")
            P.wait("tensor", "p3e2")
            P.wait("scalar", "p3e2")
            self.ln_stats([ytmp[:, c, :] for c in range(DC)], 512, self.mu2, self.r2, 0, guards=("p3e2", "p3e3"))
            mu_b2, r_b2 = self.bcast2(self.mu2, self.r2, 0, 512, guards=("p2v", "p2v4", "p3e2", "p3e3"))
            P.wait("vector", "bc")
            P.wait("vector", "dX2r")
            for c in range(DC):
                P.op("vector", lambda v, c=c, mu_b2=mu_b2: v.tensor_sub(
                    ytmp[:, c, :], ytmp[:, c, :], mu_b2),
                    reads=[ytmp[:, c, :], mu_b2], writes=[ytmp[:, c, :]], inc="p3e3")
                P.op("vector", lambda v, c=c, r_b2=r_b2: v.tensor_mul(
                    ytmp[:, c, :], ytmp[:, c, :], r_b2),
                    reads=[ytmp[:, c, :], r_b2], writes=[ytmp[:, c, :]], inc="p3e3")
                P.op("vector", lambda v, c=c, off=off: v.tensor_add(
                    self.SB[:, c, off:off + 512], x2t[:, c, :], ytmp[:, c, :]),
                    reads=[x2t[:, c, :], ytmp[:, c, :]],
                    writes=[self.SB[:, c, off:off + 512]], inc="p3e3")
            P.wait("sync", "p3e3")
        self.end("bar3")

    def phase4_moe(self):
        P = self.P
        self.begin()
        sb = self.sb
        mb1_c = sb("mb1_c", [128, E * (FFE // 128)])
        mb2_c = sb("mb2_c", [128, E * DC])
        for e in range(E):
            for c in range(FFE // 128):
                P.dma("sync", mb1_c[:, e * (FFE // 128) + c],
                      self.mb1_r[e, 128 * c:128 * (c + 1)], "dW4")
            for c in range(DC):
                P.dma("sync", mb2_c[:, e * DC + c], self.mb2_r[e, 128 * c:128 * (c + 1)], "dW4")
        xp = self.scr1
        tm_t = self.scr2[:, 0:2, :].rearrange("p a b -> p (a b)")
        zt = self.scr2[:, 2:4, :].rearrange("p a b -> p (a b)")
        S_loc = sb("S_loc", [8, T])
        toki = sb("toki", [128, 1], I32)
        wg_t = sb("wg_t", [128, DC, E])
        for c in range(DC):
            P.dma("sync", wg_t[:, c, :], self.wg[128 * c:128 * (c + 1), :], "dW4")
        P.op("vector", lambda v: v.memset(zt[:, :], 0.0), writes=[zt[:, :]], inc="p4z")
        P.wait("gpsimd", "p4z")
        for r0 in range(T // 128):
            P.dma("gpsimd", self.accd[128 * r0:128 * (r0 + 1), :], zt[:, :], "dZ")
        for r0 in range(CAP // 128):
            P.dma("gpsimd", self.accd[T + 128 * r0:T + 128 * (r0 + 1), :], zt[:, :], "dZ")
        P.dma("gpsimd", self.x3ptm[T:T + 1, 0:DM], zt[0:1, 0:DM], "dZ")
        P.dma("gpsimd", self.x3ptm[T:T + 1, DM:DM + 8], zt[1:2, 0:8], "dZ")
        for tt in range(4):
            off = 512 * tt
            chunks = [self.SB[:, c, off:off + 512] for c in range(DC)]
            self.ln_stats(chunks, 512, self.mu, self.r, 0, guards=("p2v", "p2v4", "p3v", "p3e2", "p3e3", "p4v"))
            mu_b, r_b = self.bcast2(self.mu, self.r, 0, 512, guards=("p2v", "p2v4", "p3v", "p3e2", "p3e3", "p4v"))
            P.wait("vector", "bc")
            P.wait("vector", "p4tr")
            self.normalize_into([xp[:, c, :] for c in range(DC)], chunks, mu_b, r_b, "p4v")
            P.wait("tensor", "p4v")
            P.wait("tensor", "dW4")
            pS = self.pb[2]
            for c in range(DC):
                P.op("tensor", lambda t, c=c, pS=pS: t.matmul(
                    pS[0:8, :], wg_t[:, c, :], xp[:, c, :],
                    start=(c == 0), stop=(c == DC - 1), skip_group_check=True),
                    reads=[wg_t[:, c, :], xp[:, c, :]],
                    writes=[pS[0:8, :]], inc="p4sc" if c == DC - 1 else None)
            P.wait("scalar", "p4sc")
            P.op("scalar", lambda s, off=off, pS=pS: s.activation(
                out=S_loc[:, off:off + 512], in_=pS[0:8, :], func=AF.Sigmoid),
                reads=[pS[0:8, :]], writes=[S_loc[:, off:off + 512]], inc="p4sg")
            for which in range(2):
                for qc in range(4):
                    to = off + 128 * qc
                    for c in range(DC):
                        if which == 0:
                            ssrc = xp[:, c, 128 * qc:128 * (qc + 1)]
                        else:
                            ssrc = self.SB[:, c, to:to + 128]
                        P.wait("tensor", "p4te")
                        P.op("tensor", lambda t, ssrc=ssrc: t.transpose(
                            self.pb[4][:, 0:128], ssrc, self.ident[:, :]),
                            reads=[ssrc, self.ident[:, :]],
                            writes=[self.pb[4][:, 0:128]], inc="p4tr")
                        P.wait("vector", "p4tr")
                        P.wait("vector", "dTM")
                        P.op("vector", lambda v, c=c: v.tensor_copy(
                            tm_t[:, 128 * c:128 * (c + 1)], self.pb[4][:, 0:128]),
                            reads=[self.pb[4][:, 0:128]],
                            writes=[tm_t[:, 128 * c:128 * (c + 1)]], inc="p4te")
                    P.wait("gpsimd", "p4te")
                    if which == 0:
                        P.dma("gpsimd", self.x3ptm[to:to + 128, 0:DM], tm_t[:, :], "dTM")
                    else:
                        P.dma("gpsimd", self.x3tm[to:to + 128, :], tm_t[:, :], "dTM")
        P.wait("gpsimd", "p4sg")
        P.dma("gpsimd", self.ccs_in[:, :], S_loc[:, :], "dS")
        P.wait("gpsimd", "dS")
        P.collective("AllGather", ALU.bypass, [list(range(NCORES))],
                     self.ccs_in.flatten(), self.ccs_out.flatten(), "ccS")
        P.wait("gpsimd", "ccS")
        selctx = ExitStack()
        def ssb(name, shape, dt=F32):
            return selctx.enter_context(self.nc.sbuf_tensor(name, shape, dt))
        selctxB = ExitStack()
        def bsb(name, shape, dt=F32):
            return selctxB.enter_context(self.nc.sbuf_tensor(name, shape, dt))
        ccs_v = self.ccs_out.rearrange("r e (q2 c) -> e r q2 c", q2=2)
        blo = ssb("blo", [8, 1], I32)
        bhi = ssb("bhi", [8, 1], I32)
        bmid = ssb("bmid", [8, 1], I32)
        btmp = ssb("btmp", [8, 1], I32)
        bcond = ssb("bcond", [8, 1])
        bcondi = ssb("bcondi", [8, 1], I32)
        bmidf = ssb("bmidf", [8, 1])
        S_blk = bsb("S_blk", [128, 1024])
        for e in range(E):
            P.dma("gpsimd", S_blk[16 * e:16 * (e + 1), :], ccs_v[e], "dS2")
        thr128 = bsb("thr128", [128, 1])
        cnt_p = bsb("cnt_p", [128, 1])
        bmask = bsb("bmask", [128, 1024])
        blk_onesT = bsb("blk_onesT", [8, 128])
        blk_ones = bsb("blk_ones", [128, 8])
        P.op("gpsimd", lambda g: g.memset(blk_onesT[:, :], 1.0), writes=[blk_onesT[:, :]])
        P.op("gpsimd", lambda g: g.affine_select(
            out=blk_onesT[:, :], in_=blk_onesT[:, :], pattern=[[1, 128]],
            compare_op=ALU.is_ge, fill=0.0, base=0, channel_multiplier=-16),
            reads=[blk_onesT[:, :]], writes=[blk_onesT[:, :]])
        P.op("gpsimd", lambda g: g.affine_select(
            out=blk_onesT[:, :], in_=blk_onesT[:, :], pattern=[[-1, 128]],
            compare_op=ALU.is_ge, fill=0.0, base=15, channel_multiplier=16),
            reads=[blk_onesT[:, :]], writes=[blk_onesT[:, :]], inc="p4bo")
        P.wait("tensor", "p4bo")
        P.wait("tensor", "p4te")
        P.wait("tensor", "p4tr")
        P.op("tensor", lambda t: t.transpose(self.pb[4][:, 0:8], blk_onesT[:, :],
                                             self.ident[0:8, 0:8]),
             reads=[blk_onesT[:, :], self.ident[0:8, 0:8]], writes=[self.pb[4][:, 0:8]],
             inc="p4bt")
        P.wait("vector", "p4bt")
        P.op("vector", lambda v: v.tensor_copy(blk_ones[:, :], self.pb[4][:, 0:8]),
             reads=[self.pb[4][:, 0:8]], writes=[blk_ones[:, :]], inc="p4bv")
        P.op("vector", lambda v: v.memset(blo[:, :], 0), writes=[blo[:, :]], inc="p4bv")
        P.op("vector", lambda v: v.memset(bhi[:, :], 1065353216), writes=[bhi[:, :]], inc="p4bv")
        P.wait("vector", "dS2")
        for it in range(31):
            P.op("vector", lambda v: v.tensor_add(bmid[:, :], blo[:, :], bhi[:, :]),
                 reads=[blo[:, :], bhi[:, :]], writes=[bmid[:, :]], inc="p4b")
            P.op("vector", lambda v: v.tensor_single_scalar(
                bmid[:, :], bmid[:, :], 1, ALU.arith_shift_right),
                reads=[bmid[:, :]], writes=[bmid[:, :]], inc="p4b")
            P.op("vector", lambda v: v.tensor_copy(bmidf[:, :], bmid[:, :].bitcast(F32)),
                 reads=[bmid[:, :]], writes=[bmidf[:, :]], inc="p4b")
            P.wait("tensor", "p4b")
            P.wait("tensor", "p4bx")
            P.op("tensor", lambda t: t.matmul(self.pb[6][:, 0:1], blk_onesT[:, :], bmidf[:, :],
                                              start=True, stop=True, skip_group_check=True),
                 reads=[blk_onesT[:, :], bmidf[:, :]], writes=[self.pb[6][:, 0:1]], inc="p4bm")
            P.wait("vector", "p4bm")
            P.op("vector", lambda v: v.tensor_copy(thr128[:, :], self.pb[6][:, 0:1]),
                 reads=[self.pb[6][:, 0:1]], writes=[thr128[:, :]], inc="p4bx")
            P.op("vector", lambda v: v.tensor_scalar(
                out=bmask[:, :], in0=S_blk[:, :], scalar1=thr128[:, 0:1], scalar2=None,
                op0=ALU.is_ge),
                reads=[S_blk[:, :], thr128[:, 0:1]], writes=[bmask[:, :]], inc="p4b")
            P.op("vector", lambda v: v.reduce_sum(cnt_p[:, :], bmask[:, :], AX.X),
                 reads=[bmask[:, :]], writes=[cnt_p[:, :]], inc="p4b")
            P.wait("tensor", "p4b")
            P.op("tensor", lambda t: t.matmul(self.pb[6][0:8, 1:2], blk_ones[:, :], cnt_p[:, :],
                                              start=False, stop=True, skip_group_check=True),
                 reads=[blk_ones[:, :], cnt_p[:, :]], writes=[self.pb[6][0:8, 1:2]], inc="p4bm")
            P.wait("vector", "p4bm")
            P.op("vector", lambda v: v.tensor_single_scalar(
                bcond[:, :], self.pb[6][0:8, 1:2], 4096.0, ALU.is_ge),
                reads=[self.pb[6][0:8, 1:2]], writes=[bcond[:, :]], inc="p4bx")
            P.op("vector", lambda v: v.tensor_copy(bcondi[:, :], bcond[:, :]),
                 reads=[bcond[:, :]], writes=[bcondi[:, :]], inc="p4b")
            P.op("vector", lambda v: v.tensor_sub(btmp[:, :], bmid[:, :], blo[:, :]),
                 reads=[bmid[:, :], blo[:, :]], writes=[btmp[:, :]], inc="p4b")
            P.op("vector", lambda v: v.tensor_mul(btmp[:, :], btmp[:, :], bcondi[:, :]),
                 reads=[btmp[:, :], bcondi[:, :]], writes=[btmp[:, :]], inc="p4b")
            P.op("vector", lambda v: v.tensor_add(blo[:, :], blo[:, :], btmp[:, :]),
                 reads=[blo[:, :], btmp[:, :]], writes=[blo[:, :]], inc="p4b")
            P.op("vector", lambda v: v.tensor_sub(btmp[:, :], bhi[:, :], bmid[:, :]),
                 reads=[bhi[:, :], bmid[:, :]], writes=[btmp[:, :]], inc="p4b")
            P.op("vector", lambda v: v.tensor_mul(btmp[:, :], btmp[:, :], bcondi[:, :]),
                 reads=[btmp[:, :], bcondi[:, :]], writes=[btmp[:, :]], inc="p4b")
            P.op("vector", lambda v: v.tensor_add(bhi[:, :], bmid[:, :], btmp[:, :]),
                 reads=[bmid[:, :], btmp[:, :]], writes=[bhi[:, :]], inc="p4b")
        P.op("vector", lambda v: v.tensor_copy(bmidf[:, :], blo[:, :].bitcast(F32)),
             reads=[blo[:, :]], writes=[bmidf[:, :]], inc="p4b")
        if SUBSTAGE == "p4dbg":
            P.wait("gpsimd", "p4b")
            P.wait("gpsimd", "dS2")
            P.dma("gpsimd", self.out[0:128, :], S_blk[:, :], "dOut")
            P.dma("gpsimd", self.out[200:201, 0:8], bmidf[0:8, :].rearrange("a b -> b a")
                  if False else bmidf[0:8, 0:1], "dOut")
            P.wait("gpsimd", "dOut")
            P.wait("sync", "dOut")
            selctxB.close()
            selctx.close()
            self.phase_ctx.close()
            self.phase_ctx = None
            return
        P.wait("vector", "p4b")
        P.wait("vector", "p4bx")
        P.wait("scalar", "lnR")
        P.wait("scalar", "p4b") if "p4b" in P.sems else None
        selctxB.close()
        m8 = ssb("m8", [8, T])
        LT = ssb("LT", [128, 128])
        eoffr = ssb("eoffr", [1, 8])
        eoffb = ssb("eoffb", [128, 8])
        mT = ssb("mT", [128, 8])
        gvT = ssb("gvT", [128, 8])
        inclT = ssb("inclT", [128, 8])
        desti_t = ssb("desti_t", [128, 8], I32)
        pairv = ssb("pairv", [128, 2])
        dcol = ssb("dcol", [128, 1], I32)
        P.wait("gpsimd", "p4b")
        P.wait("gpsimd", "p4bx")
        P.wait("gpsimd", "p4bm")
        P.op("gpsimd", lambda g: g.memset(LT[:, :], 1.0), writes=[LT[:, :]])
        P.op("gpsimd", lambda g: g.affine_select(
            out=LT[:, :], in_=LT[:, :], pattern=[[1, 128]],
            compare_op=ALU.is_ge, fill=0.0, base=0, channel_multiplier=-1),
            reads=[LT[:, :]], writes=[LT[:, :]])
        P.op("gpsimd", lambda g: g.iota(eoffr[:, :], [[CAP + 16, 8]], base=0,
                                        channel_multiplier=0,
                                        allow_small_or_imprecise_dtypes=True),
             writes=[eoffr[:, :]])
        P.op("gpsimd", lambda g: g.iota(toki[:, :], [[0, 1]], base=0,
                                        channel_multiplier=1),
             writes=[toki[:, :]])
        P.op("gpsimd", lambda g: g.engine_nop(), inc="p4io")
        P.wait("tensor", "p4io")
        P.op("tensor", lambda t: t.matmul(self.pb[5][:, 0:8], self.ones_row[:, :], eoffr[0:1, :],
                                          start=True, stop=True, skip_group_check=True),
             reads=[self.ones_row[:, :], eoffr[0:1, :]], writes=[self.pb[5][:, 0:8]], inc="p4eo")
        P.wait("vector", "p4eo")
        P.op("vector", lambda v: v.tensor_copy(eoffb[:, :], self.pb[5][:, 0:8]),
             reads=[self.pb[5][:, 0:8]], writes=[eoffb[:, :]], inc="p4eb")
        P.wait("vector", "p4b")
        P.wait("vector", "p4sg")
        P.wait("vector", "dS")
        P.op("vector", lambda v: v.tensor_scalar(
            out=m8[:, :], in0=S_loc[:, :], scalar1=bmidf[:, 0:1], scalar2=None, op0=ALU.is_ge),
            reads=[S_loc[:, :], bmidf[:, 0:1]], writes=[m8[:, :]], inc="p4c")
        P.op("vector", lambda v: v.tensor_mul(S_loc[:, :], S_loc[:, :], m8[:, :]),
             reads=[S_loc[:, :], m8[:, :]], writes=[S_loc[:, :]], inc="p4c")
        gcT = ssb("gcT", [128, 8])
        P.wait("tensor", "p4c")
        P.op("vector", lambda v: v.memset(pairv[:, :], 0.0), writes=[pairv[:, :]], inc="p4c2")
        P.op("vector", lambda v: v.memset(pairv[:, 0:1].bitcast(I32), T),
             reads=[pairv[:, 0:1]], writes=[pairv[:, 0:1]], inc="p4c2")
        P.wait("gpsimd", "p4c2")
        for e in range(E):
            e0 = e * (CAP + 16)
            for r0 in range(0, CAP + 16, 128):
                rw = min(128, CAP + 16 - r0)
                P.dma("gpsimd", self.pairsd[e0 + r0:e0 + r0 + rw, :], pairv[0:rw, 0:2], "dPz")
        P.wait("gpsimd", "dPz")
        P.wait("vector", "dPz")
        P.wait("tensor", "p4eb")
        P.wait("tensor", "p4sg")
        P.wait("vector", "p4io")
        for cc in range(T // 128):
            t0 = 128 * cc
            P.wait("tensor", "p4mt")
            P.op("tensor", lambda t, t0=t0: t.transpose(
                self.pb[5][:, 0:8], m8[:, t0:t0 + 128], self.ident[0:8, 0:8]),
                reads=[m8[:, t0:t0 + 128], self.ident[0:8, 0:8]],
                writes=[self.pb[5][:, 0:8]], inc="p4tm")
            P.wait("vector", "p4tm")
            P.wait("vector", "p4sct")
            P.op("vector", lambda v: v.tensor_copy(mT[:, :], self.pb[5][:, 0:8]),
                 reads=[self.pb[5][:, 0:8]], writes=[mT[:, :]], inc="p4mt")
            P.wait("tensor", "p4mt")
            P.op("tensor", lambda t, t0=t0: t.transpose(
                self.pb[5][:, 0:8], S_loc[:, t0:t0 + 128], self.ident[0:8, 0:8]),
                reads=[S_loc[:, t0:t0 + 128], self.ident[0:8, 0:8]],
                writes=[self.pb[5][:, 0:8]], inc="p4tm")
            P.wait("vector", "p4tm")
            P.wait("vector", "gdma")
            P.op("vector", lambda v: v.tensor_copy(gvT[:, :], self.pb[5][:, 0:8]),
                 reads=[self.pb[5][:, 0:8]], writes=[gvT[:, :]], inc="p4mt")
            P.wait("gpsimd", "p4mt")
            P.dma("gpsimd", self.x3ptm[t0:t0 + 128, DM:DM + 8], gvT[:, :], "gdma")
            P.wait("tensor", "p4mt")
            P.wait("tensor", "p4ds")
            P.op("tensor", lambda t: t.matmul(self.pb[6][:, 0:8], LT[:, :], mT[:, :],
                                              start=True, stop=True, skip_group_check=True),
                 reads=[LT[:, :], mT[:, :]], writes=[self.pb[6][:, 0:8]], inc="p4in")
            P.wait("vector", "p4in")
            P.op("vector", lambda v: v.tensor_copy(inclT[:, :], self.pb[6][:, 0:8]),
                 reads=[self.pb[6][:, 0:8]], writes=[inclT[:, :]], inc="p4ds")
            P.op("vector", lambda v, cc=cc: v.tensor_scalar_add(
                inclT[:, :], inclT[:, :], float(64 * cc - 1 - CAP)),
                reads=[inclT[:, :]], writes=[inclT[:, :]], inc="p4ds")
            P.op("vector", lambda v: v.tensor_mul(inclT[:, :], inclT[:, :], mT[:, :]),
                 reads=[inclT[:, :], mT[:, :]], writes=[inclT[:, :]], inc="p4ds")
            P.op("vector", lambda v: v.tensor_scalar_add(inclT[:, :], inclT[:, :], float(CAP)),
                 reads=[inclT[:, :]], writes=[inclT[:, :]], inc="p4ds")
            P.op("vector", lambda v: v.tensor_single_scalar(
                inclT[:, :], inclT[:, :], float(CAP), ALU.min),
                reads=[inclT[:, :]], writes=[inclT[:, :]], inc="p4ds")
            P.op("vector", lambda v: v.tensor_add(inclT[:, :], inclT[:, :], eoffb[:, :]),
                 reads=[inclT[:, :], eoffb[:, :]], writes=[inclT[:, :]], inc="p4ds")
            P.op("vector", lambda v: v.tensor_copy(desti_t[:, :], inclT[:, :]),
                 reads=[inclT[:, :]], writes=[desti_t[:, :]], inc="p4ds")
            for e in range(E):
                P.wait("vector", "dSct")
                P.op("vector", lambda v, cc=cc: v.tensor_single_scalar(
                    pairv[:, 0:1].bitcast(I32), toki[:, :], 128 * cc, ALU.add),
                    reads=[toki[:, :]], writes=[pairv[:, 0:1]], inc="p4ci")
                P.op("vector", lambda v, e=e: v.tensor_copy(pairv[:, 1:2], gvT[:, e:e + 1]),
                     reads=[gvT[:, e:e + 1]], writes=[pairv[:, 1:2]], inc="p4ci")
                P.op("vector", lambda v, e=e: v.tensor_copy(dcol[:, :], desti_t[:, e:e + 1]),
                     reads=[desti_t[:, e:e + 1]], writes=[dcol[:, :]], inc="p4ci")
                P.wait("gpsimd", "p4ci")
                P.indirect(self.pairsd[:, :],
                           bass.IndirectOffsetOnAxis(dcol[:, 0:1], 0),
                           pairv[:, :], None, "dSct",
                           extra_reads=[dcol[:, 0:1], pairv[:, :]], unchecked_out=True)
            P.op("vector", lambda v: v.engine_nop(), inc="p4sct")
        P.wait("gpsimd", "dSct")
        P.wait("vector", "dSct")
        P.wait("scalar", "dSct")
        P.wait("tensor", "dSct")
        if SUBSTAGE == "p4dbg2":
            for e in range(E):
                e0 = e * (CAP + 16)
                P.dma("gpsimd", self.out[0:CAP + 16, 2 * e:2 * e + 2],
                      self.pairsd[e0:e0 + CAP + 16, 0:2], "dOut",
                      allow_slow_non_contiguous=True)
            P.dma("gpsimd", self.out[1100:1108, 0:1], bmidf[0:8, 0:1], "dOut",
                  allow_slow_non_contiguous=True)
            P.dma("gpsimd", self.out[1200:1208, 0:1024], S_loc[:, 0:1024], "dOut")
            P.dma("gpsimd", self.out[1210:1218, 0:1024], S_loc[:, 1024:2048], "dOut")
            P.wait("gpsimd", "dOut")
            P.wait("sync", "dOut")
            selctx.close()
            self.phase_ctx.close()
            self.phase_ctx = None
            return
        selctx.close()
        selctx = ExitStack()
        SAf = self.SA[:, :, :].rearrange("p a b -> p (a b)")
        SBf = self.SB[:, :, :].rearrange("p a b -> p (a b)")
        ex_fm = SAf
        hid_s = SBf
        scr1f = self.scr1[:, :, :].rearrange("p a b -> p (a b)")
        scr2f = self.scr2[:, :, :].rearrange("p a b -> p (a b)")
        w1v = SAf[:, 8192:16384]
        w2v = SBf[:, 8192:16384]
        ex_tm = sb("ex_tm", [128, DM + 8])
        ot_tm = sb("ot_tm", [128, DM])
        idxcol = sb("idxcol", [128, CAP // 128], I32)

        def oacc_ap(dmc, half):
            slot = dmc * 2 + half
            if slot < 8:
                return scr1f[:, 512 * slot:512 * (slot + 1)]
            return scr2f[:, 512 * (slot - 8):512 * (slot - 8 + 1)]
        P.wait("gpsimd", "dZ")
        P.wait("gpsimd", "dTM")
        P.wait("gpsimd", "p4v")
        P.wait("gpsimd", "p4te")
        P.wait("vector", "dTM")
        P.wait("vector", "dZ")
        P.wait("vector", "dS")
        P.wait("vector", "p4sg")
        P.wait("vector", "dW4")
        P.wait("scalar", "dW4")
        phsb = sb
        gsave = sb("gsave", [128, CAP // 128])
        P.wait("gpsimd", "gdma")
        P.wait("gpsimd", "dZ")
        P.wait("gpsimd", "dTM")
        for e in range(E):
            P.wait("gpsimd", "p4gt")
            P.wait("gpsimd", "dG")
            P.wait("gpsimd", "dSC")
            P.wait("gpsimd", "dIc")
            e0 = e * (CAP + 16)
            for j in range(CAP // 128):
                P.dma("gpsimd", idxcol[:, j:j + 1],
                      self.pairsd[e0 + 128 * j:e0 + 128 * (j + 1), 0:1].bitcast(I32), "dIc",
                      allow_slow_non_contiguous=True)
            P.wait("gpsimd", "dIc")
            for j in range(CAP // 128):
                if j > 0 or e > 0:
                    P.wait("gpsimd", "p4gt")
                    P.wait("gpsimd", "p4gs")
                    P.wait("gpsimd", "dG")
                P.indirect(ex_tm[:, :], None, self.x3ptm[:, :],
                           bass.IndirectOffsetOnAxis(idxcol[:, j:j + 1], 0), "dG",
                           extra_reads=[idxcol[:, j:j + 1]])
                P.wait("tensor", "dG")
                P.wait("tensor", "p4h")
                P.wait("tensor", "p4ote")
                P.wait("vector", "dG")
                P.wait("vector", "p4ob")
                P.op("vector", lambda v, j=j, e=e: v.tensor_copy(
                    gsave[:, j:j + 1], ex_tm[:, DM + e:DM + e + 1]),
                    reads=[ex_tm[:, DM + e:DM + e + 1]], writes=[gsave[:, j:j + 1]],
                    inc="p4gs")
                for c in range(DC):
                    P.wait("tensor", "p4ge")
                    P.op("tensor", lambda t, c=c: t.transpose(
                        self.pb[4][:, 0:128], ex_tm[:, 128 * c:128 * (c + 1)], self.ident[:, :]),
                        reads=[ex_tm[:, 128 * c:128 * (c + 1)], self.ident[:, :]],
                        writes=[self.pb[4][:, 0:128]], inc="p4gt")
                    P.wait("vector", "p4gt")
                    P.op("vector", lambda v, c=c, j=j: v.tensor_copy(
                        ex_fm[:, 1024 * c + 128 * j:1024 * c + 128 * (j + 1)],
                        self.pb[4][:, 0:128]),
                        reads=[self.pb[4][:, 0:128]],
                        writes=[ex_fm[:, 1024 * c + 128 * j:1024 * c + 128 * (j + 1)]],
                        inc="p4ge")
            P.wait("tensor", "p4ge")
            for sl in range(4):
                P.wait("sync", "p4h")
                P.wait("sync", "p4w2m")
                P.wait("sync", "p4hs")
                P.wait("sync", "p4ot")
                P.wait("sync", "p4gt")
                P.wait("sync", "dW1")
                P.wait("sync", "lnS")
                P.wait("sync", "lnA")
                P.wait("sync", "lnB")
                P.wait("sync", "p4v")
                P.wait("sync", "p4tr")
                P.wait("sync", "p4sc")
                P.wait("sync", "p4ge")
                for c in range(DC):
                    P.dma("sync", w1v[:, 1024 * c:1024 * (c + 1)],
                          self.w1[e, 128 * c:128 * (c + 1), 1024 * sl:1024 * (sl + 1)],
                          "dW1")
                for fr in range(8):
                    P.dma("sync", w2v[:, 1024 * fr:1024 * (fr + 1)],
                          self.w2[e, 1024 * sl + 128 * fr:1024 * sl + 128 * (fr + 1), :], "dW1")
                P.wait("tensor", "dW1")
                for fc in range(8):
                    ph = self.pb[0 + fc % 2]
                    for half in range(2):
                        P.wait("tensor", "p4hs")
                        for c in range(DC):
                            P.op("tensor", lambda t, c=c, fc=fc, half=half, ph=ph: t.matmul(
                                ph[:, :],
                                w1v[:, 1024 * c + 128 * fc:1024 * c + 128 * (fc + 1)],
                                ex_fm[:, 1024 * c + 512 * half:1024 * c + 512 * (half + 1)],
                                start=(c == 0), stop=(c == DC - 1), skip_group_check=True),
                                reads=[w1v[:, 1024 * c + 128 * fc:1024 * c + 128 * (fc + 1)],
                                       ex_fm[:, 1024 * c + 512 * half:1024 * c + 512 * (half + 1)]],
                                writes=[ph[:, :]],
                                inc="p4h" if c == DC - 1 else None)
                        P.wait("scalar", "p4h")
                        P.wait("scalar", "p4w2m")
                        P.wait("scalar", "p4v")
                        P.wait("scalar", "p4te")
                        P.wait("scalar", "lnS")
                        P.wait("scalar", "p4tr")
                        fcg = sl * 8 + fc
                        P.op("scalar", lambda s, fc=fc, half=half, ph=ph, fcg=fcg, e=e:
                             s.activation(
                                 out=hid_s[:, 1024 * fc + 512 * half:1024 * fc + 512 * (half + 1)],
                                 in_=ph[:, :], func=AF.Gelu,
                                 bias=mb1_c[:, e * 32 + fcg:e * 32 + fcg + 1]),
                             reads=[ph[:, :], mb1_c[:, e * 32 + fcg:e * 32 + fcg + 1]],
                             writes=[hid_s[:, 1024 * fc + 512 * half:1024 * fc + 512 * (half + 1)]],
                             inc="p4hs")
                P.wait("tensor", "p4hs")
                P.wait("tensor", "p4oa")
                for dmc in range(DC):
                    for half in range(2):
                        po = self.pb[2 + (dmc + half) % 2]
                        P.wait("tensor", "p4oa")
                        P.wait("tensor", "p4ob")
                        for fc in range(8):
                            P.op("tensor", lambda t, fc=fc, dmc=dmc, half=half, po=po: t.matmul(
                                po[:, :], w2v[:, 1024 * fc + 128 * dmc:1024 * fc + 128 * (dmc + 1)],
                                hid_s[:, 1024 * fc + 512 * half:1024 * fc + 512 * (half + 1)],
                                start=(fc == 0), stop=(fc == 7), skip_group_check=True),
                                reads=[w2v[:, 1024 * fc + 128 * dmc:1024 * fc + 128 * (dmc + 1)],
                                       hid_s[:, 1024 * fc + 512 * half:1024 * fc + 512 * (half + 1)]],
                                writes=[po[:, :]], inc="p4w2m" if fc == 7 else None)
                        P.wait("vector", "p4w2m")
                        if sl == 0:
                            P.op("vector", lambda v, dmc=dmc, half=half, po=po: v.tensor_copy(
                                oacc_ap(dmc, half), po[:, :]),
                                reads=[po[:, :]], writes=[oacc_ap(dmc, half)], inc="p4oa")
                        else:
                            P.op("vector", lambda v, dmc=dmc, half=half, po=po: v.tensor_add(
                                oacc_ap(dmc, half), oacc_ap(dmc, half), po[:, :]),
                                reads=[oacc_ap(dmc, half), po[:, :]],
                                writes=[oacc_ap(dmc, half)], inc="p4oa")
            for dmc in range(DC):
                for half in range(2):
                    P.op("vector", lambda v, dmc=dmc, half=half, e=e: v.tensor_scalar_add(
                        oacc_ap(dmc, half), oacc_ap(dmc, half),
                        mb2_c[:, e * DC + dmc:e * DC + dmc + 1]),
                        reads=[oacc_ap(dmc, half), mb2_c[:, e * DC + dmc:e * DC + dmc + 1]],
                        writes=[oacc_ap(dmc, half)], inc="p4ob")
            P.wait("tensor", "p4ob")
            if e > 0:
                P.wait("gpsimd", "dSC")
            for j in range(CAP // 128):
                half = j // 4
                jj = j % 4
                for dmc in range(DC):
                    src = oacc_ap(dmc, half)[:, 128 * jj:128 * (jj + 1)]
                    P.wait("tensor", "p4ote")
                    P.op("tensor", lambda t, src=src: t.transpose(
                        self.pb[4][:, 0:128], src, self.ident[:, :]),
                        reads=[src, self.ident[:, :]],
                        writes=[self.pb[4][:, 0:128]], inc="p4ot")
                    P.wait("vector", "p4ot")
                    P.wait("vector", "dSC")
                    P.op("vector", lambda v, dmc=dmc: v.tensor_copy(
                        ot_tm[:, 128 * dmc:128 * (dmc + 1)], self.pb[4][:, 0:128]),
                        reads=[self.pb[4][:, 0:128]],
                        writes=[ot_tm[:, 128 * dmc:128 * (dmc + 1)]], inc="p4ote")
                P.wait("vector", "p4gs")
                P.op("vector", lambda v, j=j: v.tensor_scalar_mul(
                    ot_tm[:, :], ot_tm[:, :], gsave[:, j:j + 1]),
                    reads=[ot_tm[:, :], gsave[:, j:j + 1]], writes=[ot_tm[:, :]], inc="p4ote")
                P.wait("gpsimd", "p4ote")
                P.indirect(self.accd[:, :], bass.IndirectOffsetOnAxis(idxcol[:, j:j + 1], 0),
                           ot_tm[:, :], None, "dSC", compute_op=ALU.add,
                           extra_reads=[idxcol[:, j:j + 1]], unchecked_out=True)
        acc_t = ex_tm[:, 0:DM]
        x3t_t = sb("x3t_t", [128, DM])
        mcol = sb("mcol", [128, 1])
        rcol = sb("rcol", [128, 1])
        sqr = ot_tm
        P.wait("gpsimd", "dSC")
        P.wait("gpsimd", "dG")
        P.wait("gpsimd", "p4gt")
        P.wait("gpsimd", "p4ote")
        P.wait("gpsimd", "p4ob")
        P.wait("gpsimd", "dIc")
        for eng2 in ("vector", "scalar", "tensor", "sync"):
            for snm in ("dG", "dSC", "p4gt", "p4ote", "p4ge", "p4ob", "p4oa", "p4gbe", "dIc", "dAc"):
                if snm in P.sems:
                    P.wait(eng2, snm)
        for r0 in range(T // 128):
            if r0 > 0:
                P.wait("gpsimd", "p4fin")
                P.wait("gpsimd", "dOut")
                P.wait("gpsimd", "dAc")
            P.dma("gpsimd", acc_t[:, :], self.accd[128 * r0:128 * (r0 + 1), :], "dAc")
            P.dma("gpsimd", x3t_t[:, :], self.x3tm[128 * r0:128 * (r0 + 1), :], "dAc")
            P.wait("vector", "dAc")
            P.wait("vector", "dOut")
            P.op("vector", lambda v: v.reduce_sum(mcol[:, :], acc_t[:, :], AX.X),
                 reads=[acc_t[:, :]], writes=[mcol[:, :]], inc="p4f")
            P.op("vector", lambda v: v.tensor_mul(sqr[:, :], acc_t[:, :], acc_t[:, :]),
                 reads=[acc_t[:, :]], writes=[sqr[:, :]], inc="p4f")
            P.op("vector", lambda v: v.reduce_sum(rcol[:, :], sqr[:, :], AX.X),
                 reads=[sqr[:, :]], writes=[rcol[:, :]], inc="p4f")
            P.op("vector", lambda v: v.tensor_scalar_mul(mcol[:, :], mcol[:, :], 1.0 / DM),
                 reads=[mcol[:, :]], writes=[mcol[:, :]], inc="p4f")
            P.op("vector", lambda v: v.tensor_scalar_mul(rcol[:, :], rcol[:, :], 1.0 / DM),
                 reads=[rcol[:, :]], writes=[rcol[:, :]], inc="p4f")
            P.op("vector", lambda v: v.tensor_mul(sqr[:, 0:1], mcol[:, :], mcol[:, :]),
                 reads=[mcol[:, :]], writes=[sqr[:, 0:1]], inc="p4f")
            P.op("vector", lambda v: v.tensor_sub(rcol[:, :], rcol[:, :], sqr[:, 0:1]),
                 reads=[rcol[:, :], sqr[:, 0:1]], writes=[rcol[:, :]], inc="p4f")
            P.wait("scalar", "p4f")
            P.op("scalar", lambda s: s.activation(out=rcol[:, :], in_=rcol[:, :],
                                                  func=AF.Sqrt, bias=self.eps_c[:, 0:1]),
                 reads=[rcol[:, :], self.eps_c[:, 0:1]], writes=[rcol[:, :]], inc="p4fs")
            P.wait("vector", "p4fs")
            P.op("vector", lambda v: v.reciprocal(rcol[:, :], rcol[:, :]),
                 reads=[rcol[:, :]], writes=[rcol[:, :]], inc="p4f")
            P.op("vector", lambda v: v.tensor_scalar(
                out=acc_t[:, :], in0=acc_t[:, :], scalar1=mcol[:, 0:1], scalar2=rcol[:, 0:1],
                op0=ALU.subtract, op1=ALU.mult),
                reads=[acc_t[:, :], mcol[:, 0:1], rcol[:, 0:1]], writes=[acc_t[:, :]], inc="p4f2")
            P.op("vector", lambda v: v.tensor_add(acc_t[:, :], acc_t[:, :], x3t_t[:, :]),
                 reads=[acc_t[:, :], x3t_t[:, :]], writes=[acc_t[:, :]], inc="p4f2")
            P.wait("gpsimd", "p4f2")
            P.wait("gpsimd", "dAc")
            P.dma("gpsimd", self.out[128 * r0:128 * (r0 + 1), :], acc_t[:, :], "dOut")
            P.op("gpsimd", lambda g: g.engine_nop(), inc="p4fin")
        P.wait("gpsimd", "dOut")
        P.wait("sync", "dOut")
        self.phase_ctx.close()
        self.phase_ctx = None


def build_nc(stage=STAGE):
    nc = bass.Bass("TRN2", target_bir_lowering=False)
    B = Builder(nc, stage)
    B.prologue_and_phase1()
    if stage <= 1:
        B.emit_stream_out(B.SA if stage == 1 else B.SB)
    if stage >= 2:
        B.phase2_l0ffn()
        if stage == 2:
            B.emit_stream_out(B.SA)
    if stage >= 3:
        B.phase3_l1attn()
        if stage == 3:
            B.emit_stream_out(B.SB)
    if stage >= 4:
        B.phase4_moe()
    block = B.ctx.enter_context(nc.Block())
    B.P.replay(block)
    return nc, B


def make_in_maps(inputs):
    xs = np.asarray(inputs["L0_attn_x"])
    G_, NB, MD = 256, 32, 128
    pos = np.arange(G_)
    rel = pos[None, :] - pos[:, None]
    n = np.maximum(-rel, 0)
    max_exact = NB // 2
    ln_ratio = math.log(MD / max_exact)
    val_large = max_exact + (np.log(np.maximum(n, 1).astype(np.float32) / max_exact) / ln_ratio * (NB - max_exact)).astype(np.int32)
    val_large = np.minimum(val_large, NB - 1)
    bucket = np.where(n < max_exact, n, val_large)
    rel_tab = np.asarray(inputs["L0_attn_rel_table"])
    biasQK = rel_tab[bucket]
    biasT = np.ascontiguousarray(biasQK.T.reshape(2, 128, G_))

    wqkv = np.asarray(inputs["L1_attn_w_qkv"]).reshape(DM, NH, 3, HD)
    wq = np.ascontiguousarray(wqkv[:, :, 0, :].reshape(DM, DM))
    wk = np.ascontiguousarray(wqkv[:, :, 1, :].reshape(DM, DM))
    wv = np.ascontiguousarray(wqkv[:, :, 2, :].reshape(DM, DM))

    common = {
        "wh": np.asarray(inputs["L0_attn_Wh"]), "bh": np.asarray(inputs["L0_attn_bh"]),
        "wqk": np.asarray(inputs["L0_attn_Wqk"]), "bqk": np.asarray(inputs["L0_attn_bqk"]),
        "qkw": np.asarray(inputs["L0_attn_qk_w"]), "qkb": np.asarray(inputs["L0_attn_qk_b"]),
        "biasTd": biasT, "lnw": np.asarray(inputs["L0_attn_ln_w"]),
        "lnb": np.asarray(inputs["L0_attn_ln_b"]),
        "wout0": np.asarray(inputs["L0_attn_Wout"]), "bout0": np.asarray(inputs["L0_attn_bout"]),
        "u1": np.asarray(inputs["L0_ffn_U1"]), "v1": np.asarray(inputs["L0_ffn_V1"]),
        "fb1": np.asarray(inputs["L0_ffn_b1"]), "u2": np.asarray(inputs["L0_ffn_U2"]),
        "v2": np.asarray(inputs["L0_ffn_V2"]), "fb2": np.asarray(inputs["L0_ffn_b2"]),
        "wq": wq, "wk": wk, "wv": wv,
        "wo1": np.asarray(inputs["L1_attn_w_out"]), "bo1": np.asarray(inputs["L1_attn_b_out"]),
        "gam": np.asarray(inputs["L1_attn_gamma"]), "bet": np.asarray(inputs["L1_attn_beta"]),
        "wg": np.asarray(inputs["L1_ffn_Wg"]),
        "w1": np.asarray(inputs["L1_ffn_W1"]), "mb1": np.asarray(inputs["L1_ffn_b1"]),
        "w2": np.asarray(inputs["L1_ffn_W2"]), "mb2": np.asarray(inputs["L1_ffn_b2"]),
    }
    in_maps = []
    for c in range(NCORES):
        b, h = c // 2, c % 2
        m = dict(common)
        m["xT"] = np.ascontiguousarray(xs[b, h * T:(h + 1) * T, :].T)
        m["maskc"] = np.full((128, 1), 1.0 if h == 1 else 0.0, np.float32)
        in_maps.append(m)
    return in_maps


def kernel(**inputs):
    in_maps = make_in_maps(inputs)
    nc, _ = build_nc(STAGE)
    res = run_bass_kernel_spmd(nc, in_maps, core_ids=list(range(NCORES)))
    outs = [np.asarray(r["out"]).reshape(T, DM) for r in res.results]
    full = np.zeros((4, 4096, DM), np.float32)
    for c in range(NCORES):
        b, h = c // 2, c % 2
        full[b, h * T:(h + 1) * T, :] = outs[c]
    return full
